# version 13
# baseline (speedup 1.0000x reference)
import os
import numpy as np
import ml_dtypes
import concourse.bass as bass
import concourse.mybir as mybir
from concourse.bass_utils import run_bass_kernel_spmd

F32 = mybir.dt.float32
BF16 = mybir.dt.bfloat16
I32 = mybir.dt.int32
AF = mybir.ActivationFunctionType
OP = mybir.AluOpType
AX = mybir.AxisListType.X

B, L, H, Dh, DM, FF = 4, 4096, 16, 64, 1024, 3072
NT = 2048
NSLOT, TI = 4, 512
ILIST = [[0, 1, 6, 7], [2, 3, 4, 5]]
PADK = 511
WKT = 4704
EPS = 1e-5
DQ = 1.0 / 127.0


class Ctr:
    def __init__(self, es, nc, name):
        self.sem = es.enter_context(nc.semaphore(name))
        self.v = 0

    def inc(self, instr, amt=1):
        instr.then_inc(self.sem, amt)
        self.v += amt
        return instr

    def wait(self, eng, val=None):
        eng.wait_ge(self.sem, self.v if val is None else val)


def build_program(cfg=None):
    cfg = cfg or {}
    nc = bass.Bass("TRN2", target_bir_lowering=False, debug=False)
    dbg = cfg.get("debug", "")
    kq = "ExternalOutput" if "qkv" in dbg else "Internal"
    ka = "ExternalOutput" if "att" in dbg else "Internal"

    x_own = nc.dram_tensor("x_own", [NT, DM], F32, kind="ExternalInput").ap()
    x_kv = nc.dram_tensor("x_kv", [L, DM], F32, kind="ExternalInput").ap()
    w_qkv = nc.dram_tensor("w_qkv", [DM, 3 * DM], F32, kind="ExternalInput").ap()
    w_o = nc.dram_tensor("w_o", [DM, DM], F32, kind="ExternalInput").ap()
    w1 = nc.dram_tensor("w1", [32, 32], F32, kind="ExternalInput").ap()
    b1 = nc.dram_tensor("b1", [32, 1], F32, kind="ExternalInput").ap()
    w2 = nc.dram_tensor("w2", [32, 16], F32, kind="ExternalInput").ap()
    b2 = nc.dram_tensor("b2", [16, 1], F32, kind="ExternalInput").ap()
    log_p = nc.dram_tensor("log_p", [16, 1], F32, kind="ExternalInput").ap()
    log_a = nc.dram_tensor("log_a", [16, 1], F32, kind="ExternalInput").ap()
    wg_i = nc.dram_tensor("wg_i", [FF, DM], I32, kind="ExternalInput").ap()
    wu_i = nc.dram_tensor("wu_i", [FF, DM], I32, kind="ExternalInput").ap()
    wd_i = nc.dram_tensor("wd_i", [DM, FF], I32, kind="ExternalInput").ap()
    ident_in = nc.dram_tensor("ident_in", [128, 128], BF16, kind="ExternalInput").ap()
    ipack_in = nc.dram_tensor("ipack_in", [128, 64], BF16, kind="ExternalInput").ap()
    masks_in = nc.dram_tensor("masks_in", [128, 4 * TI], BF16, kind="ExternalInput").ap()
    cosq_in = nc.dram_tensor("cosq_in", [128, NT], F32, kind="ExternalInput").ap()
    sinq_in = nc.dram_tensor("sinq_in", [128, NT], F32, kind="ExternalInput").ap()
    cosk_in = nc.dram_tensor("cosk_in", [128, L], F32, kind="ExternalInput").ap()
    sink_in = nc.dram_tensor("sink_in", [128, L], F32, kind="ExternalInput").ap()
    gshift_in = nc.dram_tensor("gshift_in", [64, 1], F32, kind="ExternalInput").ap()
    par_in = nc.dram_tensor("par_in", [1, 1], I32, kind="ExternalInput").ap()
    out = nc.dram_tensor("out", [NT, DM], F32, kind="ExternalOutput").ap()

    qTs = nc.dram_tensor("qTs", [NSLOT, 8, 128, TI], BF16, kind=kq)
    kTs = nc.dram_tensor("kTs", [32, H, 64, 128], BF16, kind=kq)
    vs = nc.dram_tensor("vs", [32, 128, H * 65], BF16, kind=kq)
    avs = nc.dram_tensor("avs", [NSLOT, 65, H * TI], F32, kind=ka)
    dbg_stat = nc.dram_tensor("dbg_stat", [1, 13], F32, kind="ExternalOutput") if "stats" in dbg else None
    dbg_eb = nc.dram_tensor("dbg_eb", [64, 1], F32, kind="ExternalOutput") if "stats" in dbg else None
    dbg_wx = nc.dram_tensor("dbg_wx", [64, 16384], BF16, kind="ExternalOutput") if "stats" in dbg else None
    dbg_s = nc.dram_tensor("dbg_s", [128, 8192], BF16, kind="ExternalOutput") if "stats" in dbg else None
    wdTs = nc.dram_tensor("wdTs", [8, 24, 128, 128], BF16, kind="Internal")
    wgTs = nc.dram_tensor("wgTs", [24, 8, 128, 128], BF16, kind="Internal")
    wuTs = nc.dram_tensor("wuTs", [24, 8, 128, 128], BF16, kind="Internal")

    from contextlib import ExitStack
    with ExitStack() as es:
        sb = lambda n_, s, d: es.enter_context(nc.sbuf_tensor(n_, s, d))
        pst_ = lambda n_, s, d: es.enter_context(nc.psum_tensor(n_, s, d))
        C = lambda n_: Ctr(es, nc, n_)

        ident = sb("ident", [128, 128], BF16)
        ipack = sb("ipack", [128, 64], BF16)
        maskb = sb("maskb", [128, 4 * TI], BF16)
        w1bd = sb("w1bd", [128, 128], BF16)
        w1kbd = sb("w1kbd", [64, 128], BF16)
        w2bd = sb("w2bd", [128, 64], BF16)
        b1p = sb("b1p", [128, 1], F32)
        ebias = sb("ebias", [128, 1], F32)
        ktd = nc.dram_tensor("ktd", [64, WKT], BF16, kind="Internal")
        smalls = sb("smalls", [64, 4], F32)
        stat = sb("stat", [4, 16], F32)
        onesb = sb("onesb", [128, 128], BF16)
        psb_i = sb("psb_i", [1, 1], I32)
        big0 = sb("big0", [128, 24576], BF16)
        big1 = sb("big1", [128, 16384], BF16)
        big2 = sb("big2", [128, 8192], BF16)
        es3 = ExitStack()
        big3 = es3.enter_context(nc.sbuf_tensor("big3", [128, 8192], F32))
        kt = es3.enter_context(nc.sbuf_tensor("kt", [64, WKT], BF16))

        psS = pst_("psS", [128, 2048], F32)
        psA = pst_("psA", [128, 1024], F32)
        psFd = pst_("psFd", [128, 1024], F32)
        psS0 = psS[:, 0:512]
        psS1 = psS[:, 512:1024]
        psT = psS[:, 0:64].bitcast(BF16)
        psT64 = psS[:, 512:544].bitcast(BF16)
        psT64b = psS[:, 64:96].bitcast(BF16)
        psT2 = psS[:, 576:640].bitcast(BF16)

        cLD = C("cLD"); cST = C("cST")
        cQd = C("cQd"); cQv = C("cQv"); cQt = C("cQt"); cQs = C("cQs"); cQg = C("cQg")
        cAd = C("cAd"); cAg = C("cAg"); cAt = C("cAt"); cAv = C("cAv"); cAs = C("cAs")
        cFd = C("cFd"); cFd2 = C("cFd2"); cFt = C("cFt"); cFv = C("cFv"); cFs = C("cFs"); cFg = C("cFg")
        cPi = C("cPi")

        engs = [nc.tensor, nc.vector, nc.scalar, nc.sync, nc.gpsimd]

        def barrier(ctrs):
            for e in engs:
                for c in ctrs:
                    c.wait(e)

        def dbat(ctr, eng):
            eng.wait_ge(ctr.sem, ctr.v)

        for ap_, dst in ((ident_in, ident[:, :]), (ipack_in, ipack[:, :]),
                         (masks_in, maskb[:, :]), (par_in, psb_i[:, :])):
            cLD.inc(nc.sync.dma_start(out=dst, in_=ap_), 16)
        cLD.inc(nc.sync.dma_start(out=big3[0:32, 0:32], in_=w1), 16)
        cLD.inc(nc.sync.dma_start(out=big3[0:32, 32:48], in_=w2), 16)
        cLD.inc(nc.sync.dma_start(out=big3[0:32, 48:49], in_=b1), 16)
        cLD.inc(nc.sync.dma_start(out=big3[0:16, 49:50], in_=b2), 16)
        cLD.inc(nc.sync.dma_start(out=big3[0:16, 50:51], in_=log_p), 16)
        cLD.inc(nc.sync.dma_start(out=big3[0:16, 51:52], in_=log_a), 16)
        cLD.inc(nc.sync.dma_start(out=big3[0:64, 52:53], in_=gshift_in), 16)
        cLD.wait(nc.vector)
        cQv.inc(nc.vector.memset(onesb[:, :], 1.0))
        cQv.inc(nc.vector.memset(w1bd[:, :], 0))
        cQv.inc(nc.vector.memset(w1kbd[:, :], 0))
        cQv.inc(nc.vector.memset(w2bd[:, :], 0))
        cQv.inc(nc.vector.tensor_copy(big1[0:32, 0:32], big3[0:32, 0:32]))
        cQv.inc(nc.vector.tensor_copy(big1[0:32, 32:48], big3[0:32, 32:48]))
        cQv.wait(nc.gpsimd)
        dbat(cQg, nc.gpsimd)
        for g in range(4):
            cQg.inc(nc.gpsimd.dma_start(out=w1bd[32 * g:32 * (g + 1), 32 * g:32 * (g + 1)],
                                        in_=big1[0:32, 0:32]), 16)
            cQg.inc(nc.gpsimd.dma_start(out=w1kbd[16 * g:16 * (g + 1), 32 * g:32 * (g + 1)],
                                        in_=big1[16:32, 0:32]), 16)
            cQg.inc(nc.gpsimd.dma_start(out=w2bd[32 * g:32 * (g + 1), 16 * g:16 * (g + 1)],
                                        in_=big1[0:32, 32:48]), 16)
            cQg.inc(nc.gpsimd.dma_start(out=b1p[32 * g:32 * (g + 1), 0:1],
                                        in_=big3[0:32, 48:49]), 16)
        cQv.inc(nc.vector.tensor_scalar(big3[0:16, 53:54], big3[0:16, 50:51], 0.0, None, OP.max))
        cQv.inc(nc.vector.tensor_scalar(big3[0:16, 54:55], big3[0:16, 51:52], 0.0, None, OP.max))
        cQv.wait(nc.scalar)
        cQs.inc(nc.scalar.activation(big3[0:16, 55:56], big3[0:16, 50:51], AF.Abs))
        cQs.inc(nc.scalar.activation(big3[0:16, 56:57], big3[0:16, 51:52], AF.Abs))
        cQs.wait(nc.scalar)
        cQs.inc(nc.scalar.activation(big3[0:16, 57:58], big3[0:16, 55:56], AF.Exp, scale=-1.0))
        cQs.inc(nc.scalar.activation(big3[0:16, 58:59], big3[0:16, 56:57], AF.Exp, scale=-1.0))
        cQs.wait(nc.scalar)
        cQs.inc(nc.scalar.activation(big3[0:16, 57:58], big3[0:16, 57:58], AF.Ln, bias=1.0))
        cQs.inc(nc.scalar.activation(big3[0:16, 58:59], big3[0:16, 58:59], AF.Ln, bias=1.0))
        cQs.wait(nc.vector)
        cQv.inc(nc.vector.tensor_tensor(big3[0:16, 59:60], big3[0:16, 53:54],
                                        big3[0:16, 57:58], OP.add))
        cQv.inc(nc.vector.tensor_tensor(big3[0:16, 60:61], big3[0:16, 54:55],
                                        big3[0:16, 58:59], OP.add))
        cQv.wait(nc.vector)
        cQv.inc(nc.vector.tensor_scalar(big3[0:16, 61:62], big3[0:16, 59:60], -1.0, None, OP.mult))
        cQv.wait(nc.gpsimd)
        dbat(cQg, nc.gpsimd)
        for g in range(4):
            cQg.inc(nc.gpsimd.dma_start(out=smalls[16 * g:16 * (g + 1), 0:1],
                                        in_=big3[0:16, 61:62]), 16)
            cQg.inc(nc.gpsimd.dma_start(out=smalls[16 * g:16 * (g + 1), 1:2],
                                        in_=big3[0:16, 60:61]), 16)
        cQg.wait(nc.gpsimd)
        KTC = 1024
        cPi.inc(nc.gpsimd.iota(big3[0:64, KTC:KTC + WKT], [[1, WKT]],
                               channel_multiplier=0,
                               allow_small_or_imprecise_dtypes=True))
        cPi.wait(nc.vector)
        cQv.inc(nc.vector.tensor_scalar(big3[0:64, KTC:KTC + WKT], big3[0:64, KTC:KTC + WKT],
                                        big3[0:64, 52:53], None, OP.subtract))
        cQv.wait(nc.scalar)
        cQs.inc(nc.scalar.activation(big3[0:64, KTC:KTC + WKT], big3[0:64, KTC:KTC + WKT],
                                     AF.Abs))
        cQs.wait(nc.vector)
        cQv.inc(nc.vector.tensor_scalar(big3[0:64, KTC:KTC + WKT], big3[0:64, KTC:KTC + WKT],
                                        smalls[0:64, 1:2], None, OP.mult))
        cQv.wait(nc.scalar)
        cQs.inc(nc.scalar.activation(big3[0:64, KTC:KTC + WKT], big3[0:64, KTC:KTC + WKT],
                                     AF.Ln, bias=1.0))
        cQs.wait(nc.vector)
        cQv.inc(nc.vector.tensor_scalar(kt[0:64, 0:WKT], big3[0:64, KTC:KTC + WKT],
                                        smalls[0:64, 0:1], None, OP.mult))
        cQv.wait(nc.gpsimd)
        dbat(cQg, nc.gpsimd)
        cQg.inc(nc.gpsimd.dma_start(out=ktd.ap(), in_=kt[0:64, 0:WKT]), 16)
        cQv.inc(nc.vector.memset(stat[0:1, 0:2], 0.0))
        cQv.inc(nc.vector.memset(big3[0:64, 64:116], 0.0))
        cQv.wait(nc.scalar)
        cQs.inc(nc.scalar.activation(big3[0:32, 64:96], big3[0:32, 0:32], AF.Abs))
        cQs.inc(nc.scalar.activation(big3[0:32, 96:112], big3[0:32, 32:48], AF.Abs))
        cQs.inc(nc.scalar.activation(big3[0:32, 112:113], big3[0:32, 48:49], AF.Abs))
        cQs.inc(nc.scalar.activation(big3[0:16, 113:114], big3[0:16, 49:50], AF.Abs))
        cQv.wait(nc.vector)
        cQv.inc(nc.vector.tensor_reduce(big3[0:64, 114:115], kt[0:64, 0:WKT],
                                        AX, OP.max, apply_absolute_value=True))
        cQs.wait(nc.vector)
        cQv.wait(nc.vector)
        cQv.inc(nc.vector.tensor_copy(big2[0:64, 0:51], big3[0:64, 64:115]))
        cQv.wait(nc.tensor)
        cQt.inc(nc.tensor.matmul(psFd[0:1, 0:32], onesb[0:32, 0:1], big2[0:32, 0:32],
                                 start=True, stop=True))
        cQt.inc(nc.tensor.matmul(psFd[0:1, 32:48], onesb[0:32, 0:1], big2[0:32, 32:48],
                                 start=True, stop=True))
        cQt.inc(nc.tensor.matmul(psFd[0:1, 48:49], onesb[0:32, 0:1], big2[0:32, 48:49],
                                 start=True, stop=True))
        cQt.inc(nc.tensor.matmul(psFd[0:1, 49:50], onesb[0:16, 0:1], big2[0:16, 49:50],
                                 start=True, stop=True))
        cQt.inc(nc.tensor.transpose(psT[0:1, 0:64], big2[0:64, 50:51], ident[0:64, 0:64]))
        cQt.wait(nc.vector)
        cQv.inc(nc.vector.tensor_reduce(stat[0:1, 8:9], psFd[0:1, 0:32], AX, OP.max))
        cQv.inc(nc.vector.tensor_reduce(stat[0:1, 9:10], psFd[0:1, 32:48], AX, OP.max))
        cQv.inc(nc.vector.tensor_copy(stat[0:1, 10:11], psFd[0:1, 48:49]))
        cQv.inc(nc.vector.tensor_copy(stat[0:1, 11:12], psFd[0:1, 49:50]))
        cQv.inc(nc.vector.tensor_reduce(stat[0:1, 12:13], psT[0:1, 0:64], AX, OP.max))
        cQv.wait(nc.tensor)

        cQv.wait(nc.sync)
        for w_in, w_out, wd_layout in ((wg_i, wgTs, False), (wu_i, wuTs, False),
                                       (wd_i, wdTs, True)):
            for hc in range(24):
                dbat(cQd, nc.sync)
                if wd_layout:
                    srcap = w_in[:, 128 * hc:128 * (hc + 1)].rearrange("(m p) c -> p m c", p=128)
                else:
                    srcap = w_in[128 * hc:128 * (hc + 1), :].rearrange("p (m c) -> p m c", c=128)
                cQd.inc(nc.sync.dma_start(
                    out=big3[:, 2048:3072].bitcast(I32).rearrange("p (m c) -> p m c", c=128),
                    in_=srcap), 16)
                cQd.wait(nc.vector)
                cQv.inc(nc.vector.tensor_copy(big1[:, 0:1024], big3[:, 2048:3072].bitcast(I32)))
                cQv.wait(nc.tensor)
                for mc in range(8):
                    pt = psT if mc % 2 == 0 else psT2
                    cQv.wait(nc.tensor, cQv.v - 1)
                    cQt.inc(nc.tensor.transpose(pt, big1[:, 128 * mc:128 * (mc + 1)],
                                                ident[:, :]))
                    cQt.wait(nc.vector)
                    cQv.inc(nc.vector.tensor_copy(big1[:, 1024 + 128 * mc:1024 + 128 * (mc + 1)],
                                                  pt))
                cQv.wait(nc.gpsimd)
                dbat(cQg, nc.gpsimd)
                dst = wdTs[:, hc].transpose([1, 0, 2]) if wd_layout else                     w_out[hc].transpose([1, 0, 2])
                cQg.inc(nc.gpsimd.dma_start(
                    out=dst,
                    in_=big1[:, 1024:2048].rearrange("p (m c) -> p m c", c=128)), 16)
                cQg.wait(nc.sync)
                cQg.wait(nc.vector)

        with ExitStack() as esq:
            sbq = lambda n_, s, d: esq.enter_context(nc.sbuf_tensor(n_, s, d))
            tabf = sbq("tabf", [128, 2 * L], BF16)
            tabq = sbq("tabq", [128, 2 * NT], BF16)
            vstage = sbq("vstage", [128, 4 * H * 65], BF16)

            dbat(cQg, nc.gpsimd)
            for ap_, dst in ((cosq_in, tabq[:, 0:NT]), (sinq_in, tabq[:, NT:2 * NT]),
                             (cosk_in, tabf[:, 0:L]), (sink_in, tabf[:, L:2 * L])):
                cQg.inc(nc.gpsimd.dma_start(out=dst, in_=ap_), 16)
            cQg.wait(nc.vector)
            cLD.wait(nc.vector)
            for tq in range(4):
                for h in range(H):
                    cQv.inc(nc.vector.tensor_copy(
                        vstage[:, 1040 * tq + 65 * h + 64:1040 * tq + 65 * h + 65],
                        onesb[:, 0:1]))
            for kc in range(8):
                cQv.wait(nc.sync)
                dbat(cQd, nc.sync)
                cQd.inc(nc.sync.dma_start(out=big3[:, 0:1536],
                                          in_=w_qkv[128 * kc:128 * (kc + 1), 0:1536]), 16)
                cQd.inc(nc.sync.dma_start(out=big3[:, 1536:3072],
                                          in_=w_qkv[128 * kc:128 * (kc + 1), 1536:3072]), 16)
                cQd.wait(nc.vector)
                cQv.inc(nc.vector.tensor_copy(big0[:, 3072 * kc:3072 * (kc + 1)],
                                              big3[:, 0:3072]))

            def rope_chunk(tab, tcol, dst, pa, par):
                Ltab = tab.shape[1] // 2
                cos = tab[:, tcol:tcol + TI]
                sin = tab[:, Ltab + tcol:Ltab + tcol + TI]
                t1 = big3[:, 4096 + 1024 * par:4608 + 1024 * par]
                t2 = big3[:, 4608 + 1024 * par:5120 + 1024 * par]
                cQv.inc(nc.vector.tensor_tensor(t1, pa, cos, OP.mult))
                for q4 in range(4):
                    r = 32 * q4
                    pr = r + 32 if (q4 % 2 == 0) else r - 32
                    cQv.inc(nc.vector.tensor_tensor(t2[r:r + 32, :],
                                                    pa[pr:pr + 32, :],
                                                    sin[r:r + 32, :], OP.mult))
                cQv.wait(nc.vector)
                cQv.inc(nc.vector.tensor_tensor(dst, t1, t2, OP.add))

            def issue_xload(row0, src_ap, copy_mark):
                nc.sync.wait_ge(cQv.sem, copy_mark)
                dbat(cQd, nc.sync)
                for tq in range(4):
                    cQd.inc(nc.sync.dma_start(
                        out=big3[:, 1024 * tq:1024 * (tq + 1)],
                        in_=src_ap[row0 + 128 * tq:row0 + 128 * (tq + 1), :]), 16)

            def proj_block(row0, src_ap, ocs, kind, blk, nxt):
                cQd.wait(nc.vector)
                cQg.wait(nc.vector)
                cQv.inc(nc.vector.tensor_copy(big1[:, 0:4096], big3[:, 0:4096]))
                if nxt is not None:
                    issue_xload(nxt[0], nxt[1], cQv.v)
                cQv.wait(nc.tensor)
                tm = {}
                for idx in range(32):
                    tq, kc = idx // 8, idx % 8
                    pt = psT if idx % 2 == 0 else psT2
                    if idx >= 2:
                        nc.tensor.wait_ge(cQv.sem, tm[idx - 2])
                    cQt.inc(nc.tensor.transpose(
                        pt, big1[:, 1024 * tq + 128 * kc:1024 * tq + 128 * (kc + 1)],
                        ident[:, :]))
                    cQt.wait(nc.vector)
                    cQv.inc(nc.vector.tensor_copy(
                        big1[:, 4096 + 512 * kc + 128 * tq:4096 + 512 * kc + 128 * (tq + 1)],
                        pt))
                    tm[idx] = cQv.v
                cQv.wait(nc.tensor)
                cQs.wait(nc.vector)
                rm = {}
                for ix, oc in enumerate(ocs):
                    pa = psA[:, 512 * (ix % 2):512 * (ix % 2) + 512]
                    if ix >= 2:
                        nc.tensor.wait_ge(cQv.sem, rm[ix - 2])
                    for kc in range(8):
                        cQt.inc(nc.tensor.matmul(
                            pa, big0[:, 3072 * kc + 128 * oc:3072 * kc + 128 * (oc + 1)],
                            big1[:, 4096 + 512 * kc:4096 + 512 * (kc + 1)],
                            start=(kc == 0), stop=(kc == 7)))
                    cQt.wait(nc.vector)
                    if kind == "q":
                        rope_chunk(tabq, TI * blk, big1[:, 8192 + 512 * ix:8192 + 512 * (ix + 1)], pa, ix % 2)
                    elif ix < 8:
                        rope_chunk(tabf, TI * blk, big1[:, 8192 + 512 * ix:8192 + 512 * (ix + 1)], pa, ix % 2)
                    else:
                        cQv.inc(nc.vector.tensor_copy(
                            big1[:, 12288 + 512 * (ix - 8):12288 + 512 * (ix - 7)], pa))
                    rm[ix] = cQv.v
                srow = 0 if kind == "q" else 1
                cQv.wait(nc.scalar)
                cQs.inc(nc.scalar.activation(big2[:, 0:4096], big1[:, 8192:12288],
                                             AF.Square))
                cQs.wait(nc.tensor)
                bm = {}
                for cix in range(8):
                    pf = psFd[0:1, 512 * (cix % 2):512 * (cix % 2) + 512]
                    if cix >= 2:
                        nc.tensor.wait_ge(cQv.sem, bm[cix - 2])
                    cQt.inc(nc.tensor.matmul(pf, onesb[:, 0:1],
                                             big2[:, 512 * cix:512 * (cix + 1)],
                                             start=True, stop=True))
                    cQt.wait(nc.vector)
                    cQv.inc(nc.vector.tensor_reduce(big3[0:1, 6400 + cix:6401 + cix],
                                                    pf, AX, OP.max))
                    bm[cix] = cQv.v
                cQv.wait(nc.vector)
                cQv.inc(nc.vector.tensor_reduce(big3[0:1, 6416:6417],
                                                big3[0:1, 6400:6408], AX, OP.max))
                cQv.wait(nc.vector)
                cQv.inc(nc.vector.tensor_tensor(stat[0:1, srow:srow + 1],
                                                stat[0:1, srow:srow + 1],
                                                big3[0:1, 6416:6417], OP.max))
                if kind == "kv":
                    vm = {}
                    for vi in range(64):
                        cix, hpar, tq = vi // 8, (vi // 4) % 2, vi % 4
                        h = 2 * cix + hpar
                        pv = psT64 if vi % 2 == 0 else psT64b
                        if vi >= 2:
                            nc.tensor.wait_ge(cQv.sem, vm[vi - 2])
                        cQt.inc(nc.tensor.transpose(
                            pv,
                            big1[64 * hpar:64 * (hpar + 1),
                                 12288 + 512 * cix + 128 * tq:12288 + 512 * cix + 128 * (tq + 1)],
                            ident[64 * hpar:64 * (hpar + 1), 64 * hpar:64 * (hpar + 1)]))
                        cQt.wait(nc.vector)
                        cQv.inc(nc.vector.tensor_copy(
                            vstage[:, 1040 * tq + 65 * h:1040 * tq + 65 * h + 64],
                            pv))
                        vm[vi] = cQv.v
                cQv.wait(nc.gpsimd)
                dbat(cQg, nc.gpsimd)
                if kind == "q":
                    for ix in range(8):
                        cQg.inc(nc.gpsimd.dma_start(
                            out=qTs[blk, ix],
                            in_=big1[:, 8192 + 512 * ix:8192 + 512 * (ix + 1)]), 16)
                else:
                    for cix in range(8):
                        for hpar in range(2):
                            h = 2 * cix + hpar
                            srck = big1[64 * hpar:64 * (hpar + 1),
                                        8192 + 512 * cix:8192 + 512 * (cix + 1)]
                            cQg.inc(nc.gpsimd.dma_start(
                                out=bass.AP(kTs, ((4 * blk) * H + h) * 64 * 128,
                                            [[128, 64], [H * 64 * 128, 4], [1, 128]]),
                                in_=srck.rearrange("p (j c) -> p j c", c=128)), 16)
                    for tq in range(4):
                        cQg.inc(nc.gpsimd.dma_start(out=vs[4 * blk + tq],
                                                    in_=vstage[:, 1040 * tq:1040 * (tq + 1)]), 16)
                cQg.wait(nc.vector)

            if cfg.get("phase_q", True):
                blocks = []
                for kb in range(8):
                    blocks.append((512 * kb, x_kv, list(range(8, 24)), "kv", kb))
                for qb in range(4):
                    blocks.append((512 * qb, x_own, list(range(0, 8)), "q", qb))
                issue_xload(blocks[0][0], blocks[0][1], cQv.v)
                for bi, (row0b, srcb, ocsb, kindb, blkb) in enumerate(blocks):
                    nxt = (blocks[bi + 1][0], blocks[bi + 1][1]) if bi + 1 < len(blocks) else None
                    proj_block(row0b, srcb, ocsb, kindb, blkb, nxt)

                cQv.inc(nc.vector.tensor_tensor(stat[0:1, 2:3], stat[0:1, 0:1],
                                                stat[0:1, 1:2], OP.mult))
                cQv.wait(nc.scalar)
                cQs.inc(nc.scalar.activation(stat[0:1, 3:4], stat[0:1, 2:3], AF.Ln))
                cQs.wait(nc.scalar)
                cQs.inc(nc.scalar.activation(stat[0:1, 4:5], stat[0:1, 3:4], AF.Exp, scale=0.5))
                cQs.wait(nc.vector)
                cQv.inc(nc.vector.tensor_tensor(stat[0:1, 5:6], stat[0:1, 4:5],
                                                stat[0:1, 12:13], OP.max))
                cQv.wait(nc.vector)
                cQv.inc(nc.vector.tensor_tensor(stat[0:1, 5:6], stat[0:1, 5:6],
                                                stat[0:1, 8:9], OP.mult))
                cQv.wait(nc.vector)
                cQv.inc(nc.vector.tensor_tensor(stat[0:1, 5:6], stat[0:1, 5:6],
                                                stat[0:1, 10:11], OP.add))
                cQv.wait(nc.vector)
                cQv.inc(nc.vector.tensor_scalar(stat[0:1, 5:6], stat[0:1, 5:6], 1.0, None, OP.max))
                cQv.wait(nc.vector)
                cQv.inc(nc.vector.tensor_tensor(stat[0:1, 6:7], stat[0:1, 5:6],
                                                stat[0:1, 9:10], OP.mult))
                cQv.wait(nc.vector)
                cQv.inc(nc.vector.tensor_tensor(stat[0:1, 6:7], stat[0:1, 6:7],
                                                stat[0:1, 11:12], OP.add))
                cQv.wait(nc.vector)
                cQv.inc(nc.vector.tensor_tensor(stat[0:1, 7:8], stat[0:1, 4:5],
                                                stat[0:1, 6:7], OP.add))
                cQv.wait(nc.vector)
                cQv.inc(nc.vector.tensor_copy(big2[0:1, 8:9], stat[0:1, 7:8]))
                cQv.wait(nc.tensor)
                cQt.inc(nc.tensor.matmul(psA[:, 600:601], onesb[0:1, 0:128],
                                         big2[0:1, 8:9], start=True, stop=True))
                cQt.wait(nc.gpsimd)
                dbat(cQg, nc.gpsimd)
                for g in range(8):
                    cQg.inc(nc.gpsimd.dma_start(out=ebias[16 * g:16 * (g + 1), 0:1],
                                                in_=big3[0:16, 49:50]), 16)
                cQg.wait(nc.vector)
                cQv.inc(nc.vector.tensor_tensor(ebias[:, 0:1], ebias[:, 0:1],
                                                psA[:, 600:601], OP.subtract))
                if dbg_stat is not None:
                    cQv.wait(nc.sync)
                    dbat(cQd, nc.sync)
                    cQd.inc(nc.sync.dma_start(out=dbg_stat.ap(), in_=stat[0:1, 0:13]), 16)
                    cQd.inc(nc.sync.dma_start(out=dbg_eb.ap(), in_=ebias[0:64, :]), 16)

        es3.close()
        avbuf = es.enter_context(nc.sbuf_tensor("avbuf", [65, H * TI], F32))

        with ExitStack() as esa:
            sba = lambda n_, s, d: esa.enter_context(nc.sbuf_tensor(n_, s, d))
            wxb = sba("wxb", [128, 8192], BF16)
            x2t = sba("x2t", [128, 16384], BF16)
            ktv2 = [sba("ktv0", [128, 1024], BF16), sba("ktv1", [128, 1024], BF16),
                    sba("ktv2b", [128, 1024], BF16)]
            vsb2 = [sba("vsb0", [128, H * 65], BF16), sba("vsb1", [128, H * 65], BF16),
                    sba("vsb2b", [128, H * 65], BF16)]
            qtb = sba("qtb", [128, 8 * TI], BF16)
            sKV = [C("sKV0"), C("sKV1"), C("sKV2")]
            sXSa = C("sXSa"); sXSb = C("sXSb"); sXSc = C("sXSc")
            sTIa = C("sTIa"); sTIb = C("sTIb"); sTIc = C("sTIc")

            SB = 0
            XB = 8192
            BP = 24576
            psV2 = [psFd[0:65, 0:512], psFd[0:65, 512:1024]]
            psAh = [psA[:, 0:512], psA[:, 512:1024]]
            psSp = [psS[:, 0:1024], psS[:, 1024:2048]]

            bk = {"kv_free": [0, 0], "kv_tot": [0, 0], "x_free": 0,
                  "sx_rec": (0, 0, 0), "ti_rec": (0, 0, 0), "w2_free_t": 0,
                  "w2_free_v": 0, "av_done": 0, "ti_mark": [(0, 0)] * 16,
                  "mmax": 4, "evac_marks": [0] * 8, "evac_last": 0}
            bk["kv_free"] = [0, 0, 0]
            bk["kv_tot"] = [0, 0, 0]
            bk["x_free2"] = [0, 0]
            sTI3g = (sTIa, sTIb, sTIc)
            q3g = (nc.sync, nc.scalar, nc.gpsimd)

            def issue_kv(m):
                par = m % 3
                nc.sync.wait_ge(cAt.sem, bk["kv_free"][par])
                dbat(sKV[par], nc.sync)
                sKV[par].inc(nc.sync.dma_start(
                    out=ktv2[par][0:64, :],
                    in_=bass.AP(kTs, (m * H) * 8192, [[128, 64], [2 * 8192, 8], [1, 128]])), 16)
                sKV[par].inc(nc.sync.dma_start(
                    out=ktv2[par][64:128, :],
                    in_=bass.AP(kTs, (m * H + 1) * 8192, [[128, 64], [2 * 8192, 8], [1, 128]])), 16)
                sKV[par].inc(nc.sync.dma_start(out=vsb2[par][:, :], in_=vs[m]), 16)
                bk["kv_tot"][par] = sKV[par].v

            tp_count = [0]

            def sc_compute(tglob, m):
                par = m % 3
                ktv = ktv2[par]
                nc.tensor.wait_ge(sKV[par].sem, bk["kv_tot"][par])
                nc.vector.wait_ge(sXSa.sem, bk["sx_rec"][0])
                nc.vector.wait_ge(sXSb.sem, bk["sx_rec"][1])
                nc.vector.wait_ge(sXSc.sem, bk["sx_rec"][2])
                prev_marks = bk["evac_marks"]
                marks = []
                for hp in range(8):
                    pp = psSp[hp % 2]
                    if hp >= 2:
                        nc.tensor.wait_ge(cAv.sem, marks[hp - 2])
                    else:
                        nc.tensor.wait_ge(cAv.sem, prev_marks[6 + hp])
                    cAt.inc(nc.tensor.matmul(pp[:, 0:512], ktv[0:64, 128 * hp:128 * (hp + 1)],
                                             qtb[0:64, 512 * hp:512 * (hp + 1)],
                                             start=True, stop=True, tile_position=(0, 0)))
                    cAt.inc(nc.tensor.matmul(pp[:, 512:1024], ktv[64:128, 128 * hp:128 * (hp + 1)],
                                             qtb[64:128, 512 * hp:512 * (hp + 1)],
                                             start=True, stop=True, tile_position=(64, 0)))
                    cAt.wait(nc.vector)
                    cAv.inc(nc.vector.tensor_copy(big0[:, SB + 1024 * hp:SB + 1024 * (hp + 1)],
                                                  pp[:, :]))
                    marks.append(cAv.v)
                bk["evac_marks"] = marks
                bk["evac_last"] = cAv.v

            def sc_dma(tglob, m):
                delta = 512 * tglob - 128 * m
                xpar = m % 2
                dbat(sXSa, nc.sync)
                dbat(sXSb, nc.scalar)
                dbat(sXSc, nc.gpsimd)
                for q_ in q3g:
                    q_.wait_ge(cAt.sem, bk["x_free2"][xpar])
                    q_.wait_ge(cAv.sem, bk["evac_last"])
                if xpar == 0:
                    def xap(g, h):
                        return bass.AP(big0, (32 * g + h) * BP + XB,
                                       [[BP, 1], [512, 32], [1, 512]])
                    def xkap(g, bh):
                        return bass.AP(big0, (32 * g + 16) * BP + XB + 512 * 16 * bh,
                                       [[BP, 16], [512, 16], [1, 512]])
                else:
                    def xap(g, h):
                        return bass.AP(x2t, (32 * g + h) * 16384,
                                       [[16384, 1], [512, 32], [1, 512]])
                    def xkap(g, bh):
                        return bass.AP(x2t, (32 * g + 16) * 16384 + 512 * 16 * bh,
                                       [[16384, 16], [512, 16], [1, 512]])
                for hp in range(8):
                    for hh in range(2):
                        h = 2 * hp + hh
                        for g in range(4):
                            sm, q_ = ((sXSa, nc.sync),
                                      (sXSc, nc.gpsimd))[(8 * hp + 4 * hh + g) % 2]
                            sm.inc(q_.dma_start(
                                out=xap(g, h),
                                in_=big0[32 * g:32 * (g + 1),
                                         SB + 512 * h:SB + 512 * (h + 1)]), 16)
                for g in range(4):
                    for bh in range(2):
                        sm, q_ = ((sXSa, nc.sync), (sXSc, nc.gpsimd),
                                  (sXSc, nc.gpsimd), (sXSa, nc.sync),
                                  (sXSc, nc.gpsimd), (sXSa, nc.sync),
                                  (sXSa, nc.sync), (sXSc, nc.gpsimd))[2 * g + bh]
                        sm.inc(q_.dma_start(
                            out=xkap(g, bh),
                            in_=bass.AP(ktd, 16 * g * WKT + (delta + PADK) - 16 * bh,
                                        [[WKT, 16], [-1, 16], [1, 512]])), 16)
                bk["sx_rec"] = (sXSa.v, sXSb.v, sXSc.v)

            def mlp_stage(tglob, m):
                par = m % 2
                delta = 512 * tglob - 128 * m
                nc.tensor.wait_ge(sXSa.sem, bk["sx_rec"][0])
                nc.tensor.wait_ge(sXSb.sem, bk["sx_rec"][1])
                nc.tensor.wait_ge(sXSc.sem, bk["sx_rec"][2])
                g_base = cAs.v
                xv = (lambda b: big0[:, XB + 512 * b:XB + 512 * (b + 1)]) \
                    if m % 2 == 0 else (lambda b: x2t[:, 512 * b:512 * (b + 1)])
                for b in range(32):
                    if b >= 2:
                        nc.tensor.wait_ge(cAs.sem, g_base + b - 1)
                    cAt.inc(nc.tensor.matmul(
                        psAh[b % 2], w1bd[:, :], xv(b),
                        start=True, stop=True))
                    cAt.wait(nc.scalar)
                    cAs.inc(nc.scalar.activation(big1[:, 512 * b:512 * (b + 1)],
                                                 psAh[b % 2], AF.Gelu, bias=b1p[:, 0:1]))
                e_base = cAs.v
                nc.tensor.wait_ge(cAv.sem, bk["av_done"])
                nc.scalar.wait_ge(sTIa.sem, bk["ti_rec"][0])
                nc.scalar.wait_ge(sTIb.sem, bk["ti_rec"][1])
                nc.scalar.wait_ge(sTIc.sem, bk["ti_rec"][2])
                for bp in range(16):
                    if bp >= 2:
                        nc.tensor.wait_ge(cAs.sem, e_base + bp - 1)
                    pF = psFd[:, 512 * (bp % 2):512 * (bp % 2) + 512]
                    for e in range(2):
                        b = bp + 16 * e
                        cAt.inc(nc.tensor.matmul(pF[64 * e:64 * (e + 1), :], w2bd[:, :],
                                                 big1[:, 512 * b:512 * (b + 1)],
                                                 start=True, stop=False,
                                                 tile_position=(0, 64 * e)))
                        cAt.inc(nc.tensor.matmul(pF[64 * e:64 * (e + 1), :], ipack[:, :],
                                                 xv(b),
                                                 start=False, stop=True,
                                                 tile_position=(0, 64 * e)))
                    cAt.wait(nc.scalar)
                    cAs.inc(nc.scalar.activation(wxb[:, 512 * bp:512 * (bp + 1)],
                                                 pF, AF.Exp, bias=ebias[:, 0:1]))
                bk["x_free2"][m % 2] = cAt.v
                w2 = big2
                for q_ in q3g:
                    q_.wait_ge(cAs.sem, cAs.v)
                    q_.wait_ge(cAt.sem, bk["w2_free_t"])
                    q_.wait_ge(cAv.sem, bk["w2_free_v"])
                ti_mark = []
                ti_q = (0, 2, 1, 0, 2, 1, 0, 2, 1, 0, 2, 1, 0, 2, 0, 2)
                for h in range(H):
                    qi = ti_q[h]
                    sm = sTI3g[qi]
                    q_ = q3g[qi]
                    q_.wait_ge(sm.sem, sm.v)
                    for g in range(4):
                        for e in range(2):
                            sm.inc(q_.dma_start(
                                out=w2[32 * g + 16 * e:32 * g + 16 * e + 16,
                                       512 * h:512 * (h + 1)],
                                in_=bass.AP(wxb, (64 * e + 16 * g + h) * 8192,
                                            [[8192, 1], [512, 16], [1, 512]])), 16)
                    ti_mark.append((qi, sm.v))
                bk["ti_mark"] = ti_mark
                bk["ti_rec"] = (sTIa.v, sTIb.v, sTIc.v)
                tp_count[0] += 1

            def av_stage(tglob, m):
                par = m % 3
                vsb = vsb2[par]
                w2 = big2
                diag = m >= 4 * tglob
                o = m - 4 * tglob
                av_base = cAv.v
                ti_mark = bk["ti_mark"]
                for h in range(H):
                    ti, tv = ti_mark[h]
                    if diag:
                        nc.vector.wait_ge(sTI3g[ti].sem, tv)
                        cAv.inc(nc.vector.tensor_tensor(
                            w2[:, 512 * h:512 * (h + 1)], w2[:, 512 * h:512 * (h + 1)],
                            maskb[:, 512 * o:512 * (o + 1)], OP.mult))
                        nc.tensor.wait_ge(cAv.sem, cAv.v)
                    else:
                        nc.tensor.wait_ge(sTI3g[ti].sem, tv)
                        if h >= 2:
                            nc.tensor.wait_ge(cAv.sem, av_base + h - 1)
                    cAt.inc(nc.tensor.matmul(psV2[h % 2], vsb[:, 65 * h:65 * (h + 1)],
                                             w2[:, 512 * h:512 * (h + 1)],
                                             start=True, stop=True))
                    cAt.wait(nc.vector)
                    cAv.inc(nc.vector.tensor_tensor(avbuf[:, 512 * h:512 * (h + 1)],
                                                    avbuf[:, 512 * h:512 * (h + 1)],
                                                    psV2[h % 2], OP.add))
                bk["av_done"] = cAv.v
                bk["w2_free_v"] = cAv.v
                bk["w2_free_t"] = cAt.v
                bk["kv_free"][par] = cAt.v
                if m + 3 < bk["mmax"]:
                    issue_kv(m + 3)

            if cfg.get("phase_a", True):
                cQv.inc(nc.vector.memset(big0[:, XB:XB + 16384], 0.0))
                cQv.inc(nc.vector.memset(x2t[:, :], 0.0))
                barrier([cLD, cQd, cQv, cQt, cQs, cQg])
                fixed = cfg.get("fixed_arm", None)
                if fixed is None:
                    idxs = [nc.values_load(psb_i[0:1, 0:1], engines=[e.engine]) for e in engs]
                    arm_iter = nc.Switch(engines=engs, index=idxs, n=2)
                else:
                    arm_iter = iter([fixed])
                snap = None
                all_ctrs = (cAd, cAt, cAv, cAs, sKV[0], sKV[1], sKV[2],
                            sXSa, sXSb, sXSc, sTIa, sTIb, sTIc)
                for arm in arm_iter:
                    import copy as _copy
                    if arm == 0:
                        snap = (tuple(c.v for c in all_ctrs), _copy.deepcopy(bk))
                    else:
                        for c, v0 in zip(all_ctrs, snap[0]):
                            c.v = v0
                        bk.clear(); bk.update(_copy.deepcopy(snap[1]))
                    for s in range(NSLOT):
                        tglob = ILIST[arm][s]
                        cAd.wait(nc.vector)
                        cAv.inc(nc.vector.memset(avbuf[:, :], 0.0))
                        cAt.wait(nc.sync)
                        dbat(cAd, nc.sync)
                        for oc in range(8):
                            cAd.inc(nc.sync.dma_start(out=qtb[:, 512 * oc:512 * (oc + 1)],
                                                      in_=qTs[s, oc]), 16)
                        cAd.wait(nc.tensor)
                        cAv.wait(nc.tensor)
                        mmax = 4 * (tglob + 1)
                        bk["mmax"] = mmax
                        issue_kv(0)
                        issue_kv(1)
                        issue_kv(2)
                        sc_compute(tglob, 0)
                        sc_dma(tglob, 0)
                        for m in range(mmax):
                            if m + 1 < mmax:
                                sc_compute(tglob, m + 1)
                            if m >= 1:
                                av_stage(tglob, m - 1)
                            if m + 1 < mmax:
                                sc_dma(tglob, m + 1)
                            mlp_stage(tglob, m)
                        av_stage(tglob, mmax - 1)
                        cAv.wait(nc.sync)
                        dbat(cAd, nc.sync)
                        cAd.inc(nc.sync.dma_start(out=avs[s], in_=avbuf[:, :]), 16)
                arm_end = tuple(c.v for c in all_ctrs)

        if cfg.get("phase_f", True):
            barrier([cAd, cAg, cAt, cAv, cAs])
            big3 = es.enter_context(nc.sbuf_tensor("big3f", [128, 8192], F32))
            for kc in range(8):
                cFv.wait(nc.sync)
                dbat(cFd, nc.sync)
                cFd.inc(nc.sync.dma_start(out=big3[:, 0:1024],
                                          in_=w_o[128 * kc:128 * (kc + 1), :]), 16)
                cFd.wait(nc.vector)
                cFv.inc(nc.vector.tensor_copy(big0[:, 1024 * kc:1024 * (kc + 1)],
                                              big3[:, 0:1024]))

            def layernorm(tq):
                xt = big3[:, 1024 * tq:1024 * (tq + 1)]
                mtmp = big3[:, 4096 + 2 * tq:4096 + 2 * tq + 1]
                vtmp = big3[:, 4096 + 2 * tq + 1:4096 + 2 * tq + 2]
                sq = big3[:, 4160:5184]
                cFv.wait(nc.vector)
                cFv.inc(nc.vector.tensor_reduce(mtmp, xt, AX, OP.add))
                cFv.wait(nc.vector)
                cFv.inc(nc.vector.tensor_scalar(mtmp, mtmp, 1.0 / DM, None, OP.mult))
                cFv.wait(nc.vector)
                cFv.inc(nc.vector.tensor_scalar(xt, xt, mtmp, None, OP.subtract))
                cFv.wait(nc.vector)
                cFv.inc(nc.vector.tensor_tensor(sq, xt, xt, OP.mult))
                cFv.wait(nc.vector)
                cFv.inc(nc.vector.tensor_reduce(vtmp, sq, AX, OP.add))
                cFv.wait(nc.vector)
                cFv.inc(nc.vector.tensor_scalar(vtmp, vtmp, 1.0 / DM, EPS, OP.mult, OP.add))
                cFv.wait(nc.scalar)
                cFs.inc(nc.scalar.activation(vtmp, vtmp, AF.Ln))
                cFs.wait(nc.scalar)
                cFs.inc(nc.scalar.activation(vtmp, vtmp, AF.Exp, scale=-0.5))
                cFs.wait(nc.vector)
                cFv.inc(nc.vector.tensor_scalar(xt, xt, vtmp, None, OP.mult))

            for s in range(NSLOT):
                cFv.wait(nc.sync)
                dbat(cFd, nc.sync)
                cFd.inc(nc.sync.dma_start(out=avbuf[:, :], in_=avs[s]), 16)
                cFd.wait(nc.vector)
                nc.vector.wait_ge(cFt.sem, cFt.v)
                cFv.wait(nc.vector)
                for kq in range(4):
                    rr, cc = 32 * (kq // 2), 4096 + 2048 * (kq % 2)
                    cFv.inc(nc.vector.reciprocal(big3[rr:rr + 1, cc:cc + 2048],
                                                 avbuf[64:65, 2048 * kq:2048 * (kq + 1)]))
                cFv.wait(nc.vector)
                for kq in range(4):
                    rr, cc = 32 * (kq // 2), 4096 + 2048 * (kq % 2)
                    cFv.inc(nc.vector.tensor_copy(big2[rr:rr + 1, cc:cc + 2048],
                                                  big3[rr:rr + 1, cc:cc + 2048]))
                cFv.wait(nc.tensor)
                nb = cFv.v
                for h in range(H):
                    kq = h // 4
                    rr, cc = 32 * (kq // 2), 4096 + 2048 * (kq % 2) + 512 * (h % 4)
                    pAv = psA[0:64, 0:512] if h % 2 == 0 else psA[0:64, 512:1024]
                    if h >= 2:
                        nc.tensor.wait_ge(cFv.sem, nb + h - 1)
                    cFt.inc(nc.tensor.matmul(pAv, onesb[rr:rr + 1, 0:64],
                                             big2[rr:rr + 1, cc:cc + 512],
                                             start=True, stop=True))
                    cFt.wait(nc.vector)
                    cFv.inc(nc.vector.tensor_tensor(big2[0:64, 512 * h:512 * (h + 1)],
                                                    avbuf[0:64, 512 * h:512 * (h + 1)],
                                                    pAv, OP.mult))
                cFv.wait(nc.gpsimd)
                dbat(cFg, nc.gpsimd)
                for c in range(8):
                    cFg.inc(nc.gpsimd.dma_start(out=big1[0:64, 512 * c:512 * (c + 1)],
                                                in_=big2[0:64, 1024 * c:1024 * c + 512]), 16)
                    cFg.inc(nc.gpsimd.dma_start(out=big1[64:128, 512 * c:512 * (c + 1)],
                                                in_=big2[0:64, 1024 * c + 512:1024 * (c + 1)]), 16)
                cFg.wait(nc.tensor)
                dbat(cFd, nc.sync)
                dbat(cST, nc.sync)
                for tq in range(4):
                    cFd.inc(nc.sync.dma_start(
                        out=big3[:, 1024 * tq:1024 * (tq + 1)],
                        in_=x_own[512 * s + 128 * tq:512 * s + 128 * (tq + 1), :]), 16)
                cFd.wait(nc.vector)
                cFd.wait(nc.tensor)
                for mc in range(8):
                    for kc in range(8):
                        cFt.inc(nc.tensor.matmul(
                            psA[:, 512:1024],
                            big0[:, (kc * 8 + mc) * 128:(kc * 8 + mc + 1) * 128],
                            big1[:, 512 * kc:512 * (kc + 1)],
                            start=(kc == 0), stop=(kc == 7)))
                    cFt.wait(nc.vector)
                    cFv.inc(nc.vector.tensor_copy(
                        big2[:, 4096 + 512 * mc:4096 + 512 * (mc + 1)], psA[:, 512:1024]))
                    cFv.wait(nc.tensor)
                for mc in range(8):
                    for tq in range(4):
                        cFt.inc(nc.tensor.transpose(
                            psT,
                            big2[:, 4096 + 512 * mc + 128 * tq:4096 + 512 * mc + 128 * (tq + 1)],
                            ident[:, :]))
                        cFt.wait(nc.vector)
                        cFv.inc(nc.vector.tensor_tensor(
                            big3[:, 1024 * tq + 128 * mc:1024 * tq + 128 * (mc + 1)],
                            big3[:, 1024 * tq + 128 * mc:1024 * tq + 128 * (mc + 1)],
                            psT, OP.add))
                        cFv.wait(nc.tensor)
                for tq in range(4):
                    layernorm(tq)
                    cFv.wait(nc.vector)
                    cFv.inc(nc.vector.tensor_copy(big1[:, 1024 * tq:1024 * (tq + 1)],
                                                  big3[:, 1024 * tq:1024 * (tq + 1)]))
                cFv.wait(nc.tensor)
                for tq in range(4):
                    for kc in range(8):
                        cFv.wait(nc.tensor)
                        cFt.inc(nc.tensor.transpose(
                            psT,
                            big1[:, 1024 * tq + 128 * kc:1024 * tq + 128 * (kc + 1)],
                            ident[:, :]))
                        cFt.wait(nc.vector)
                        cFv.inc(nc.vector.tensor_copy(
                            big1[:, 4096 + 512 * kc + 128 * tq:4096 + 512 * kc + 128 * (tq + 1)],
                            psT))
                cFv.wait(nc.sync)
                cFv.wait(nc.tensor)
                sil_base = cFs.v
                mul_base = cFv.v
                wt_rd = [cFt.v, cFt.v]
                for hc in range(24):
                    par = hc % 2
                    sm = (cFd, cFd2)[par]
                    nc.sync.wait_ge(cFt.sem, wt_rd[par])
                    dbat(sm, nc.sync)
                    sm.inc(nc.sync.dma_start(out=big2[:, 2048 * par:2048 * par + 1024],
                                             in_=wgTs[hc].transpose([1, 0, 2])), 16)
                    sm.inc(nc.sync.dma_start(out=big2[:, 2048 * par + 1024:2048 * (par + 1)],
                                             in_=wuTs[hc].transpose([1, 0, 2])), 16)
                    nc.tensor.wait_ge(sm.sem, sm.v)
                    pG = psA[:, 0:512] if par == 0 else psFd[:, 0:512]
                    pU = psA[:, 512:1024] if par == 0 else psFd[:, 512:1024]
                    if hc >= 2:
                        nc.tensor.wait_ge(cFs.sem, sil_base + hc - 1)
                        nc.tensor.wait_ge(cFv.sem, mul_base + hc - 1)
                    for kc in range(8):
                        cFt.inc(nc.tensor.matmul(pG,
                                                 big2[:, 2048 * par + 128 * kc:2048 * par + 128 * (kc + 1)],
                                                 big1[:, 4096 + 512 * kc:4096 + 512 * (kc + 1)],
                                                 start=(kc == 0), stop=(kc == 7)))
                    for kc in range(8):
                        cFt.inc(nc.tensor.matmul(pU,
                                                 big2[:, 2048 * par + 1024 + 128 * kc:2048 * par + 1024 + 128 * (kc + 1)],
                                                 big1[:, 4096 + 512 * kc:4096 + 512 * (kc + 1)],
                                                 start=(kc == 0), stop=(kc == 7)))
                    wt_rd[par] = cFt.v
                    cFt.wait(nc.scalar)
                    cFs.inc(nc.scalar.activation(big2[:, 4096 + 512 * par:4608 + 512 * par],
                                                 pG, AF.Silu, scale=DQ))
                    cFs.wait(nc.vector)
                    cFv.inc(nc.vector.tensor_tensor(big0[:, 8192 + 512 * hc:8192 + 512 * (hc + 1)],
                                                    big2[:, 4096 + 512 * par:4608 + 512 * par],
                                                    pU, OP.mult))
                cFv.wait(nc.tensor)
                cFs.wait(nc.tensor)
                dn_mm = [cFt.v, cFt.v]
                dn_ev = [cFv.v, cFv.v]
                for mc in range(8):
                    par = mc % 2
                    sm = (cFd, cFd2)[par]
                    nc.sync.wait_ge(cFt.sem, dn_mm[par])
                    dbat(sm, nc.sync)
                    sm.inc(nc.sync.dma_start(
                        out=big1[:, 8192 + 3072 * par:8192 + 3072 * par + 3072],
                        in_=wdTs[mc].transpose([1, 0, 2])), 16)
                    nc.tensor.wait_ge(sm.sem, sm.v)
                    nc.tensor.wait_ge(cFv.sem, dn_ev[par])
                    pD = psS[:, 1024 + 512 * par:1536 + 512 * par]
                    for hc in range(24):
                        cFt.inc(nc.tensor.matmul(
                            pD,
                            big1[:, 8192 + 3072 * par + 128 * hc:8192 + 3072 * par + 128 * (hc + 1)],
                            big0[:, 8192 + 512 * hc:8192 + 512 * (hc + 1)],
                            start=(hc == 0), stop=(hc == 23)))
                    dn_mm[par] = cFt.v
                    cFt.wait(nc.vector)
                    cFv.inc(nc.vector.tensor_scalar(big2[:, 1024 + 512 * mc:1024 + 512 * (mc + 1)],
                                                    pD, DQ * DQ, None, OP.mult))
                    dn_ev[par] = cFv.v
                for mc in range(8):
                    for tq in range(4):
                        cFt.inc(nc.tensor.transpose(
                            psT,
                            big2[:, 1024 + 512 * mc + 128 * tq:1024 + 512 * mc + 128 * (tq + 1)],
                            ident[:, :]))
                        cFt.wait(nc.vector)
                        cFv.inc(nc.vector.tensor_tensor(
                            big3[:, 1024 * tq + 128 * mc:1024 * tq + 128 * (mc + 1)],
                            big3[:, 1024 * tq + 128 * mc:1024 * tq + 128 * (mc + 1)],
                            psT, OP.add))
                        cFv.wait(nc.tensor)
                for tq in range(4):
                    layernorm(tq)
                    cFv.wait(nc.sync)
                    cST.inc(nc.sync.dma_start(
                        out=out[512 * s + 128 * tq:512 * s + 128 * (tq + 1), :],
                        in_=big3[:, 1024 * tq:1024 * (tq + 1)]), 16)
            cST.wait(nc.sync)

    return nc


def _rope_tables(positions, scale):
    inv = 1.0 / (10000.0 ** (np.arange(0, Dh, 2) / (Dh * 1.0)))
    ang = positions[None, :] * np.concatenate([inv, inv])[:, None]
    cos = np.cos(ang) * scale
    sin = np.sin(ang) * scale
    ss = np.concatenate([-sin[0:32], sin[32:64]], 0)
    cosT = np.concatenate([cos, cos], 0).astype(np.float32)
    ssT = np.concatenate([ss, ss], 0).astype(np.float32)
    return np.ascontiguousarray(cosT), np.ascontiguousarray(ssT)


def _host_inputs(L0_attn_x, L0_attn_w_qkv, L0_attn_w_o, L0_attn_log_p, L0_attn_log_a,
                 L0_attn_w1, L0_attn_b1, L0_attn_w2, L0_attn_b2,
                 L0_ffn_w_gate, L0_ffn_w_up, L0_ffn_w_down):
    x = np.ascontiguousarray(np.asarray(L0_attn_x, dtype=np.float32))
    ident = np.eye(128, dtype=ml_dtypes.bfloat16)
    ipack = np.zeros((128, 64), dtype=ml_dtypes.bfloat16)
    for g in range(4):
        for hh in range(16):
            ipack[32 * g + hh, 16 * g + hh] = 1
            ipack[32 * g + 16 + hh, 16 * g + hh] = 1
    masks = np.zeros((128, 4 * TI), dtype=ml_dtypes.bfloat16)
    jj = np.arange(128)[:, None]
    iix = np.arange(TI)[None, :]
    for o in range(4):
        masks[:, TI * o:TI * (o + 1)] = (iix >= jj + 128 * o).astype(ml_dtypes.bfloat16)
    gshift = np.array([PADK + 32 * (r // 16) for r in range(64)], np.float32).reshape(64, 1)
    cosk, sink = _rope_tables(np.arange(L, dtype=np.float32), 1.0)

    shared = dict(
        w_qkv=np.ascontiguousarray(np.asarray(L0_attn_w_qkv, np.float32)),
        w_o=np.ascontiguousarray(np.asarray(L0_attn_w_o, np.float32)),
        w1=np.ascontiguousarray(np.asarray(L0_attn_w1, np.float32)),
        b1=np.ascontiguousarray(np.asarray(L0_attn_b1, np.float32).reshape(32, 1)),
        w2=np.ascontiguousarray(np.asarray(L0_attn_w2, np.float32)),
        b2=np.ascontiguousarray(np.asarray(L0_attn_b2, np.float32).reshape(16, 1)),
        log_p=np.ascontiguousarray(np.asarray(L0_attn_log_p, np.float32).reshape(16, 1)),
        log_a=np.ascontiguousarray(np.asarray(L0_attn_log_a, np.float32).reshape(16, 1)),
        wg_i=np.ascontiguousarray(np.asarray(L0_ffn_w_gate, np.int32)),
        wu_i=np.ascontiguousarray(np.asarray(L0_ffn_w_up, np.int32)),
        wd_i=np.ascontiguousarray(np.asarray(L0_ffn_w_down, np.int32)),
        ident_in=ident, ipack_in=ipack, masks_in=masks,
        cosk_in=cosk, sink_in=sink, gshift_in=gshift,
    )
    in_maps = []
    for c in range(8):
        b, half = c // 2, c % 2
        own_rows = np.concatenate([np.arange(512 * t, 512 * (t + 1)) for t in ILIST[half]])
        cosq, sinq = _rope_tables(own_rows.astype(np.float32), 0.125)
        im = dict(shared)
        im.update(x_own=np.ascontiguousarray(x[b][own_rows]),
                  x_kv=np.ascontiguousarray(x[b]),
                  cosq_in=cosq, sinq_in=sinq,
                  par_in=np.array([[half]], np.int32))
        in_maps.append(im)
    return in_maps


def kernel(**inputs):
    in_maps = _host_inputs(**inputs)
    nc = build_program({})
    res = run_bass_kernel_spmd(nc, in_maps, core_ids=list(range(8)))
    outp = np.zeros((B, L, DM), np.float32)
    for c in range(8):
        b, half = c // 2, c % 2
        o = res.results[c]["out"]
        for s, t in enumerate(ILIST[half]):
            outp[b, 512 * t:512 * (t + 1)] = o[512 * s:512 * (s + 1)]
    return outp


# version 14
# speedup vs baseline: 1.0923x; 1.0923x over previous
import os
import numpy as np
import ml_dtypes
import concourse.bass as bass
import concourse.mybir as mybir
from concourse.bass_utils import run_bass_kernel_spmd

F32 = mybir.dt.float32
BF16 = mybir.dt.bfloat16
I32 = mybir.dt.int32
AF = mybir.ActivationFunctionType
OP = mybir.AluOpType
AX = mybir.AxisListType.X

B, L, H, Dh, DM, FF = 4, 4096, 16, 64, 1024, 3072
NT = 2048
NSLOT, TI = 4, 512
ILIST = [[0, 1, 6, 7], [2, 3, 4, 5]]
PADK = 511
WKT = 4704
EPS = 1e-5
DQ = 1.0 / 127.0


class Ctr:
    def __init__(self, es, nc, name):
        self.sem = es.enter_context(nc.semaphore(name))
        self.v = 0

    def inc(self, instr, amt=1):
        instr.then_inc(self.sem, amt)
        self.v += amt
        return instr

    def wait(self, eng, val=None):
        eng.wait_ge(self.sem, self.v if val is None else val)


def build_program(cfg=None):
    cfg = cfg or {}
    nc = bass.Bass("TRN2", target_bir_lowering=False, debug=False)
    dbg = cfg.get("debug", "")
    kq = "ExternalOutput" if "qkv" in dbg else "Internal"
    ka = "ExternalOutput" if "att" in dbg else "Internal"

    x_own = nc.dram_tensor("x_own", [NT, DM], F32, kind="ExternalInput").ap()
    x_kv = nc.dram_tensor("x_kv", [L, DM], F32, kind="ExternalInput").ap()
    w_qkv = nc.dram_tensor("w_qkv", [DM, 3 * DM], F32, kind="ExternalInput").ap()
    w_o = nc.dram_tensor("w_o", [DM, DM], F32, kind="ExternalInput").ap()
    w1 = nc.dram_tensor("w1", [32, 32], F32, kind="ExternalInput").ap()
    b1 = nc.dram_tensor("b1", [32, 1], F32, kind="ExternalInput").ap()
    w2 = nc.dram_tensor("w2", [32, 16], F32, kind="ExternalInput").ap()
    b2 = nc.dram_tensor("b2", [16, 1], F32, kind="ExternalInput").ap()
    log_p = nc.dram_tensor("log_p", [16, 1], F32, kind="ExternalInput").ap()
    log_a = nc.dram_tensor("log_a", [16, 1], F32, kind="ExternalInput").ap()
    wg_i = nc.dram_tensor("wg_i", [FF, DM], I32, kind="ExternalInput").ap()
    wu_i = nc.dram_tensor("wu_i", [FF, DM], I32, kind="ExternalInput").ap()
    wd_i = nc.dram_tensor("wd_i", [DM, FF], I32, kind="ExternalInput").ap()
    ident_in = nc.dram_tensor("ident_in", [128, 128], BF16, kind="ExternalInput").ap()
    ipack_in = nc.dram_tensor("ipack_in", [128, 64], BF16, kind="ExternalInput").ap()
    masks_in = nc.dram_tensor("masks_in", [128, 4 * TI], BF16, kind="ExternalInput").ap()
    cosq_in = nc.dram_tensor("cosq_in", [128, NT], F32, kind="ExternalInput").ap()
    sinq_in = nc.dram_tensor("sinq_in", [128, NT], F32, kind="ExternalInput").ap()
    cosk_in = nc.dram_tensor("cosk_in", [128, L], F32, kind="ExternalInput").ap()
    sink_in = nc.dram_tensor("sink_in", [128, L], F32, kind="ExternalInput").ap()
    gshift_in = nc.dram_tensor("gshift_in", [64, 1], F32, kind="ExternalInput").ap()
    par_in = nc.dram_tensor("par_in", [1, 1], I32, kind="ExternalInput").ap()
    out = nc.dram_tensor("out", [NT, DM], F32, kind="ExternalOutput").ap()

    qTs = nc.dram_tensor("qTs", [NSLOT, 8, 128, TI], BF16, kind=kq)
    kTs = nc.dram_tensor("kTs", [32, H, 64, 128], BF16, kind=kq)
    vs = nc.dram_tensor("vs", [32, 128, H * 65], BF16, kind=kq)
    avs = nc.dram_tensor("avs", [NSLOT, 65, H * TI], F32, kind=ka)
    dbg_stat = nc.dram_tensor("dbg_stat", [1, 13], F32, kind="ExternalOutput") if "stats" in dbg else None
    dbg_eb = nc.dram_tensor("dbg_eb", [64, 1], F32, kind="ExternalOutput") if "stats" in dbg else None
    dbg_wx = nc.dram_tensor("dbg_wx", [64, 16384], BF16, kind="ExternalOutput") if "stats" in dbg else None
    dbg_s = nc.dram_tensor("dbg_s", [128, 8192], BF16, kind="ExternalOutput") if "stats" in dbg else None
    wdTs = nc.dram_tensor("wdTs", [8, 24, 128, 128], BF16, kind="Internal")
    wgTs = nc.dram_tensor("wgTs", [24, 8, 128, 128], BF16, kind="Internal")
    wuTs = nc.dram_tensor("wuTs", [24, 8, 128, 128], BF16, kind="Internal")

    from contextlib import ExitStack
    with ExitStack() as es:
        sb = lambda n_, s, d: es.enter_context(nc.sbuf_tensor(n_, s, d))
        pst_ = lambda n_, s, d: es.enter_context(nc.psum_tensor(n_, s, d))
        C = lambda n_: Ctr(es, nc, n_)

        ident = sb("ident", [128, 128], BF16)
        ipack = sb("ipack", [128, 64], BF16)
        maskb = sb("maskb", [128, 4 * TI], BF16)
        w1bd = sb("w1bd", [128, 128], BF16)
        w1kbd = sb("w1kbd", [64, 128], BF16)
        w2bd = sb("w2bd", [128, 64], BF16)
        b1p = sb("b1p", [128, 1], F32)
        ebias = sb("ebias", [128, 1], F32)
        ktd = nc.dram_tensor("ktd", [64, WKT], BF16, kind="Internal")
        smalls = sb("smalls", [64, 4], F32)
        stat = sb("stat", [4, 16], F32)
        onesb = sb("onesb", [128, 128], BF16)
        psb_i = sb("psb_i", [1, 1], I32)
        big0 = sb("big0", [128, 24576], BF16)
        big1 = sb("big1", [128, 16384], BF16)
        big2 = sb("big2", [128, 8192], BF16)
        es3 = ExitStack()
        big3 = es3.enter_context(nc.sbuf_tensor("big3", [128, 8192], F32))
        kt = es3.enter_context(nc.sbuf_tensor("kt", [64, WKT], BF16))

        psS = pst_("psS", [128, 2048], F32)
        psA = pst_("psA", [128, 1024], F32)
        psFd = pst_("psFd", [128, 1024], F32)
        psS0 = psS[:, 0:512]
        psS1 = psS[:, 512:1024]
        psT = psS[:, 0:64].bitcast(BF16)
        psT64 = psS[:, 512:544].bitcast(BF16)
        psT64b = psS[:, 64:96].bitcast(BF16)
        psT2 = psS[:, 576:640].bitcast(BF16)

        cLD = C("cLD"); cST = C("cST")
        cQd = C("cQd"); cQv = C("cQv"); cQt = C("cQt"); cQs = C("cQs"); cQg = C("cQg")
        cAd = C("cAd"); cAg = C("cAg"); cAt = C("cAt"); cAv = C("cAv"); cAs = C("cAs")
        cFd = C("cFd"); cFd2 = C("cFd2"); cFt = C("cFt"); cFv = C("cFv"); cFs = C("cFs"); cFg = C("cFg")
        cPi = C("cPi")

        engs = [nc.tensor, nc.vector, nc.scalar, nc.sync, nc.gpsimd]

        def barrier(ctrs):
            for e in engs:
                for c in ctrs:
                    c.wait(e)

        def dbat(ctr, eng):
            eng.wait_ge(ctr.sem, ctr.v)

        for ap_, dst in ((ident_in, ident[:, :]), (ipack_in, ipack[:, :]),
                         (masks_in, maskb[:, :]), (par_in, psb_i[:, :])):
            cLD.inc(nc.sync.dma_start(out=dst, in_=ap_), 16)
        cLD.inc(nc.sync.dma_start(out=big3[0:32, 0:32], in_=w1), 16)
        cLD.inc(nc.sync.dma_start(out=big3[0:32, 32:48], in_=w2), 16)
        cLD.inc(nc.sync.dma_start(out=big3[0:32, 48:49], in_=b1), 16)
        cLD.inc(nc.sync.dma_start(out=big3[0:16, 49:50], in_=b2), 16)
        cLD.inc(nc.sync.dma_start(out=big3[0:16, 50:51], in_=log_p), 16)
        cLD.inc(nc.sync.dma_start(out=big3[0:16, 51:52], in_=log_a), 16)
        cLD.inc(nc.sync.dma_start(out=big3[0:64, 52:53], in_=gshift_in), 16)
        cLD.wait(nc.vector)
        cQv.inc(nc.vector.memset(onesb[:, :], 1.0))
        cQv.inc(nc.vector.memset(w1bd[:, :], 0))
        cQv.inc(nc.vector.memset(w1kbd[:, :], 0))
        cQv.inc(nc.vector.memset(w2bd[:, :], 0))
        cQv.inc(nc.vector.tensor_copy(big1[0:32, 0:32], big3[0:32, 0:32]))
        cQv.inc(nc.vector.tensor_copy(big1[0:32, 32:48], big3[0:32, 32:48]))
        cQv.wait(nc.gpsimd)
        dbat(cQg, nc.gpsimd)
        for g in range(4):
            cQg.inc(nc.gpsimd.dma_start(out=w1bd[32 * g:32 * (g + 1), 32 * g:32 * (g + 1)],
                                        in_=big1[0:32, 0:32]), 16)
            cQg.inc(nc.gpsimd.dma_start(out=w1kbd[16 * g:16 * (g + 1), 32 * g:32 * (g + 1)],
                                        in_=big1[16:32, 0:32]), 16)
            cQg.inc(nc.gpsimd.dma_start(out=w2bd[32 * g:32 * (g + 1), 16 * g:16 * (g + 1)],
                                        in_=big1[0:32, 32:48]), 16)
            cQg.inc(nc.gpsimd.dma_start(out=b1p[32 * g:32 * (g + 1), 0:1],
                                        in_=big3[0:32, 48:49]), 16)
        cQv.inc(nc.vector.tensor_scalar(big3[0:16, 53:54], big3[0:16, 50:51], 0.0, None, OP.max))
        cQv.inc(nc.vector.tensor_scalar(big3[0:16, 54:55], big3[0:16, 51:52], 0.0, None, OP.max))
        cQv.wait(nc.scalar)
        cQs.inc(nc.scalar.activation(big3[0:16, 55:56], big3[0:16, 50:51], AF.Abs))
        cQs.inc(nc.scalar.activation(big3[0:16, 56:57], big3[0:16, 51:52], AF.Abs))
        cQs.wait(nc.scalar)
        cQs.inc(nc.scalar.activation(big3[0:16, 57:58], big3[0:16, 55:56], AF.Exp, scale=-1.0))
        cQs.inc(nc.scalar.activation(big3[0:16, 58:59], big3[0:16, 56:57], AF.Exp, scale=-1.0))
        cQs.wait(nc.scalar)
        cQs.inc(nc.scalar.activation(big3[0:16, 57:58], big3[0:16, 57:58], AF.Ln, bias=1.0))
        cQs.inc(nc.scalar.activation(big3[0:16, 58:59], big3[0:16, 58:59], AF.Ln, bias=1.0))
        cQs.wait(nc.vector)
        cQv.inc(nc.vector.tensor_tensor(big3[0:16, 59:60], big3[0:16, 53:54],
                                        big3[0:16, 57:58], OP.add))
        cQv.inc(nc.vector.tensor_tensor(big3[0:16, 60:61], big3[0:16, 54:55],
                                        big3[0:16, 58:59], OP.add))
        cQv.wait(nc.vector)
        cQv.inc(nc.vector.tensor_scalar(big3[0:16, 61:62], big3[0:16, 59:60], -1.0, None, OP.mult))
        cQv.wait(nc.gpsimd)
        dbat(cQg, nc.gpsimd)
        for g in range(4):
            cQg.inc(nc.gpsimd.dma_start(out=smalls[16 * g:16 * (g + 1), 0:1],
                                        in_=big3[0:16, 61:62]), 16)
            cQg.inc(nc.gpsimd.dma_start(out=smalls[16 * g:16 * (g + 1), 1:2],
                                        in_=big3[0:16, 60:61]), 16)
        cQg.wait(nc.gpsimd)
        KTC = 1024
        cPi.inc(nc.gpsimd.iota(big3[0:64, KTC:KTC + WKT], [[1, WKT]],
                               channel_multiplier=0,
                               allow_small_or_imprecise_dtypes=True))
        cPi.wait(nc.vector)
        cQv.inc(nc.vector.tensor_scalar(big3[0:64, KTC:KTC + WKT], big3[0:64, KTC:KTC + WKT],
                                        big3[0:64, 52:53], None, OP.subtract))
        cQv.wait(nc.scalar)
        cQs.inc(nc.scalar.activation(big3[0:64, KTC:KTC + WKT], big3[0:64, KTC:KTC + WKT],
                                     AF.Abs))
        cQs.wait(nc.vector)
        cQv.inc(nc.vector.tensor_scalar(big3[0:64, KTC:KTC + WKT], big3[0:64, KTC:KTC + WKT],
                                        smalls[0:64, 1:2], None, OP.mult))
        cQv.wait(nc.scalar)
        cQs.inc(nc.scalar.activation(big3[0:64, KTC:KTC + WKT], big3[0:64, KTC:KTC + WKT],
                                     AF.Ln, bias=1.0))
        cQs.wait(nc.vector)
        cQv.inc(nc.vector.tensor_scalar(kt[0:64, 0:WKT], big3[0:64, KTC:KTC + WKT],
                                        smalls[0:64, 0:1], None, OP.mult))
        cQv.wait(nc.gpsimd)
        dbat(cQg, nc.gpsimd)
        cQg.inc(nc.gpsimd.dma_start(out=ktd.ap(), in_=kt[0:64, 0:WKT]), 16)
        cQv.inc(nc.vector.memset(stat[0:1, 0:2], 0.0))
        cQv.inc(nc.vector.memset(big3[0:64, 64:116], 0.0))
        cQv.wait(nc.scalar)
        cQs.inc(nc.scalar.activation(big3[0:32, 64:96], big3[0:32, 0:32], AF.Abs))
        cQs.inc(nc.scalar.activation(big3[0:32, 96:112], big3[0:32, 32:48], AF.Abs))
        cQs.inc(nc.scalar.activation(big3[0:32, 112:113], big3[0:32, 48:49], AF.Abs))
        cQs.inc(nc.scalar.activation(big3[0:16, 113:114], big3[0:16, 49:50], AF.Abs))
        cQv.wait(nc.vector)
        cQv.inc(nc.vector.tensor_reduce(big3[0:64, 114:115], kt[0:64, 0:WKT],
                                        AX, OP.max, apply_absolute_value=True))
        cQs.wait(nc.vector)
        cQv.wait(nc.vector)
        cQv.inc(nc.vector.tensor_copy(big2[0:64, 0:51], big3[0:64, 64:115]))
        cQv.wait(nc.tensor)
        cQt.inc(nc.tensor.matmul(psFd[0:1, 0:32], onesb[0:32, 0:1], big2[0:32, 0:32],
                                 start=True, stop=True))
        cQt.inc(nc.tensor.matmul(psFd[0:1, 32:48], onesb[0:32, 0:1], big2[0:32, 32:48],
                                 start=True, stop=True))
        cQt.inc(nc.tensor.matmul(psFd[0:1, 48:49], onesb[0:32, 0:1], big2[0:32, 48:49],
                                 start=True, stop=True))
        cQt.inc(nc.tensor.matmul(psFd[0:1, 49:50], onesb[0:16, 0:1], big2[0:16, 49:50],
                                 start=True, stop=True))
        cQt.inc(nc.tensor.transpose(psT[0:1, 0:64], big2[0:64, 50:51], ident[0:64, 0:64]))
        cQt.wait(nc.vector)
        cQv.inc(nc.vector.tensor_reduce(stat[0:1, 8:9], psFd[0:1, 0:32], AX, OP.max))
        cQv.inc(nc.vector.tensor_reduce(stat[0:1, 9:10], psFd[0:1, 32:48], AX, OP.max))
        cQv.inc(nc.vector.tensor_copy(stat[0:1, 10:11], psFd[0:1, 48:49]))
        cQv.inc(nc.vector.tensor_copy(stat[0:1, 11:12], psFd[0:1, 49:50]))
        cQv.inc(nc.vector.tensor_reduce(stat[0:1, 12:13], psT[0:1, 0:64], AX, OP.max))
        cQv.wait(nc.tensor)

        cQv.wait(nc.sync)
        for w_in, w_out, wd_layout in ((wg_i, wgTs, False), (wu_i, wuTs, False),
                                       (wd_i, wdTs, True)):
            for hc in range(24):
                dbat(cQd, nc.sync)
                if wd_layout:
                    srcap = w_in[:, 128 * hc:128 * (hc + 1)].rearrange("(m p) c -> p m c", p=128)
                else:
                    srcap = w_in[128 * hc:128 * (hc + 1), :].rearrange("p (m c) -> p m c", c=128)
                cQd.inc(nc.sync.dma_start(
                    out=big3[:, 2048:3072].bitcast(I32).rearrange("p (m c) -> p m c", c=128),
                    in_=srcap), 16)
                cQd.wait(nc.vector)
                cQv.inc(nc.vector.tensor_copy(big1[:, 0:1024], big3[:, 2048:3072].bitcast(I32)))
                cQv.wait(nc.tensor)
                for mc in range(8):
                    pt = psT if mc % 2 == 0 else psT2
                    cQv.wait(nc.tensor, cQv.v - 1)
                    cQt.inc(nc.tensor.transpose(pt, big1[:, 128 * mc:128 * (mc + 1)],
                                                ident[:, :]))
                    cQt.wait(nc.vector)
                    cQv.inc(nc.vector.tensor_copy(big1[:, 1024 + 128 * mc:1024 + 128 * (mc + 1)],
                                                  pt))
                cQv.wait(nc.gpsimd)
                dbat(cQg, nc.gpsimd)
                dst = wdTs[:, hc].transpose([1, 0, 2]) if wd_layout else                     w_out[hc].transpose([1, 0, 2])
                cQg.inc(nc.gpsimd.dma_start(
                    out=dst,
                    in_=big1[:, 1024:2048].rearrange("p (m c) -> p m c", c=128)), 16)
                cQg.wait(nc.sync)
                cQg.wait(nc.vector)

        with ExitStack() as esq:
            sbq = lambda n_, s, d: esq.enter_context(nc.sbuf_tensor(n_, s, d))
            tabf = sbq("tabf", [128, 2 * L], BF16)
            tabq = sbq("tabq", [128, 2 * NT], BF16)
            vstage = sbq("vstage", [128, 4 * H * 65], BF16)

            dbat(cQg, nc.gpsimd)
            for ap_, dst in ((cosq_in, tabq[:, 0:NT]), (sinq_in, tabq[:, NT:2 * NT]),
                             (cosk_in, tabf[:, 0:L]), (sink_in, tabf[:, L:2 * L])):
                cQg.inc(nc.gpsimd.dma_start(out=dst, in_=ap_), 16)
            cQg.wait(nc.vector)
            cLD.wait(nc.vector)
            for tq in range(4):
                for h in range(H):
                    cQv.inc(nc.vector.tensor_copy(
                        vstage[:, 1040 * tq + 65 * h + 64:1040 * tq + 65 * h + 65],
                        onesb[:, 0:1]))
            for kc in range(8):
                cQv.wait(nc.sync)
                dbat(cQd, nc.sync)
                cQd.inc(nc.sync.dma_start(out=big3[:, 0:1536],
                                          in_=w_qkv[128 * kc:128 * (kc + 1), 0:1536]), 16)
                cQd.inc(nc.sync.dma_start(out=big3[:, 1536:3072],
                                          in_=w_qkv[128 * kc:128 * (kc + 1), 1536:3072]), 16)
                cQd.wait(nc.vector)
                cQv.inc(nc.vector.tensor_copy(big0[:, 3072 * kc:3072 * (kc + 1)],
                                              big3[:, 0:3072]))

            def rope_chunk(tab, tcol, dst, pa, par):
                Ltab = tab.shape[1] // 2
                cos = tab[:, tcol:tcol + TI]
                sin = tab[:, Ltab + tcol:Ltab + tcol + TI]
                t1 = big3[:, 4096 + 1024 * par:4608 + 1024 * par]
                t2 = big3[:, 4608 + 1024 * par:5120 + 1024 * par]
                cQv.inc(nc.vector.tensor_tensor(t1, pa, cos, OP.mult))
                for q4 in range(4):
                    r = 32 * q4
                    pr = r + 32 if (q4 % 2 == 0) else r - 32
                    cQv.inc(nc.vector.tensor_tensor(t2[r:r + 32, :],
                                                    pa[pr:pr + 32, :],
                                                    sin[r:r + 32, :], OP.mult))
                cQv.wait(nc.vector)
                cQv.inc(nc.vector.tensor_tensor(dst, t1, t2, OP.add))

            def issue_xload(row0, src_ap, copy_mark):
                nc.sync.wait_ge(cQv.sem, copy_mark)
                dbat(cQd, nc.sync)
                for tq in range(4):
                    cQd.inc(nc.sync.dma_start(
                        out=big3[:, 1024 * tq:1024 * (tq + 1)],
                        in_=src_ap[row0 + 128 * tq:row0 + 128 * (tq + 1), :]), 16)

            def proj_block(row0, src_ap, ocs, kind, blk, nxt):
                cQd.wait(nc.vector)
                cQg.wait(nc.vector)
                cQv.inc(nc.vector.tensor_copy(big1[:, 0:4096], big3[:, 0:4096]))
                if nxt is not None:
                    issue_xload(nxt[0], nxt[1], cQv.v)
                cQv.wait(nc.tensor)
                tm = {}
                for idx in range(32):
                    tq, kc = idx // 8, idx % 8
                    pt = psT if idx % 2 == 0 else psT2
                    if idx >= 2:
                        nc.tensor.wait_ge(cQv.sem, tm[idx - 2])
                    cQt.inc(nc.tensor.transpose(
                        pt, big1[:, 1024 * tq + 128 * kc:1024 * tq + 128 * (kc + 1)],
                        ident[:, :]))
                    cQt.wait(nc.vector)
                    cQv.inc(nc.vector.tensor_copy(
                        big1[:, 4096 + 512 * kc + 128 * tq:4096 + 512 * kc + 128 * (tq + 1)],
                        pt))
                    tm[idx] = cQv.v
                cQv.wait(nc.tensor)
                cQs.wait(nc.vector)
                rm = {}
                for ix, oc in enumerate(ocs):
                    pa = psA[:, 512 * (ix % 2):512 * (ix % 2) + 512]
                    if ix >= 2:
                        nc.tensor.wait_ge(cQv.sem, rm[ix - 2])
                    for kc in range(8):
                        cQt.inc(nc.tensor.matmul(
                            pa, big0[:, 3072 * kc + 128 * oc:3072 * kc + 128 * (oc + 1)],
                            big1[:, 4096 + 512 * kc:4096 + 512 * (kc + 1)],
                            start=(kc == 0), stop=(kc == 7)))
                    cQt.wait(nc.vector)
                    if kind == "q":
                        rope_chunk(tabq, TI * blk, big1[:, 8192 + 512 * ix:8192 + 512 * (ix + 1)], pa, ix % 2)
                    elif ix < 8:
                        rope_chunk(tabf, TI * blk, big1[:, 8192 + 512 * ix:8192 + 512 * (ix + 1)], pa, ix % 2)
                    else:
                        cQv.inc(nc.vector.tensor_copy(
                            big1[:, 12288 + 512 * (ix - 8):12288 + 512 * (ix - 7)], pa))
                    rm[ix] = cQv.v
                srow = 0 if kind == "q" else 1
                cQv.wait(nc.scalar)
                cQs.inc(nc.scalar.activation(big2[:, 0:4096], big1[:, 8192:12288],
                                             AF.Square))
                cQs.wait(nc.tensor)
                bm = {}
                for cix in range(8):
                    pf = psFd[0:1, 512 * (cix % 2):512 * (cix % 2) + 512]
                    if cix >= 2:
                        nc.tensor.wait_ge(cQv.sem, bm[cix - 2])
                    cQt.inc(nc.tensor.matmul(pf, onesb[:, 0:1],
                                             big2[:, 512 * cix:512 * (cix + 1)],
                                             start=True, stop=True))
                    cQt.wait(nc.vector)
                    cQv.inc(nc.vector.tensor_reduce(big3[0:1, 6400 + cix:6401 + cix],
                                                    pf, AX, OP.max))
                    bm[cix] = cQv.v
                cQv.wait(nc.vector)
                cQv.inc(nc.vector.tensor_reduce(big3[0:1, 6416:6417],
                                                big3[0:1, 6400:6408], AX, OP.max))
                cQv.wait(nc.vector)
                cQv.inc(nc.vector.tensor_tensor(stat[0:1, srow:srow + 1],
                                                stat[0:1, srow:srow + 1],
                                                big3[0:1, 6416:6417], OP.max))
                if kind == "kv":
                    vm = {}
                    for vi in range(64):
                        cix, hpar, tq = vi // 8, (vi // 4) % 2, vi % 4
                        h = 2 * cix + hpar
                        pv = psT64 if vi % 2 == 0 else psT64b
                        if vi >= 2:
                            nc.tensor.wait_ge(cQv.sem, vm[vi - 2])
                        cQt.inc(nc.tensor.transpose(
                            pv,
                            big1[64 * hpar:64 * (hpar + 1),
                                 12288 + 512 * cix + 128 * tq:12288 + 512 * cix + 128 * (tq + 1)],
                            ident[64 * hpar:64 * (hpar + 1), 64 * hpar:64 * (hpar + 1)]))
                        cQt.wait(nc.vector)
                        cQv.inc(nc.vector.tensor_copy(
                            vstage[:, 1040 * tq + 65 * h:1040 * tq + 65 * h + 64],
                            pv))
                        vm[vi] = cQv.v
                cQv.wait(nc.gpsimd)
                dbat(cQg, nc.gpsimd)
                if kind == "q":
                    for ix in range(8):
                        cQg.inc(nc.gpsimd.dma_start(
                            out=qTs[blk, ix],
                            in_=big1[:, 8192 + 512 * ix:8192 + 512 * (ix + 1)]), 16)
                else:
                    for cix in range(8):
                        for hpar in range(2):
                            h = 2 * cix + hpar
                            srck = big1[64 * hpar:64 * (hpar + 1),
                                        8192 + 512 * cix:8192 + 512 * (cix + 1)]
                            cQg.inc(nc.gpsimd.dma_start(
                                out=bass.AP(kTs, ((4 * blk) * H + h) * 64 * 128,
                                            [[128, 64], [H * 64 * 128, 4], [1, 128]]),
                                in_=srck.rearrange("p (j c) -> p j c", c=128)), 16)
                    for tq in range(4):
                        cQg.inc(nc.gpsimd.dma_start(out=vs[4 * blk + tq],
                                                    in_=vstage[:, 1040 * tq:1040 * (tq + 1)]), 16)
                cQg.wait(nc.vector)

            if cfg.get("phase_q", True):
                blocks = []
                for kb in range(8):
                    blocks.append((512 * kb, x_kv, list(range(8, 24)), "kv", kb))
                for qb in range(4):
                    blocks.append((512 * qb, x_own, list(range(0, 8)), "q", qb))
                issue_xload(blocks[0][0], blocks[0][1], cQv.v)
                for bi, (row0b, srcb, ocsb, kindb, blkb) in enumerate(blocks):
                    nxt = (blocks[bi + 1][0], blocks[bi + 1][1]) if bi + 1 < len(blocks) else None
                    proj_block(row0b, srcb, ocsb, kindb, blkb, nxt)

                cQv.inc(nc.vector.tensor_tensor(stat[0:1, 2:3], stat[0:1, 0:1],
                                                stat[0:1, 1:2], OP.mult))
                cQv.wait(nc.scalar)
                cQs.inc(nc.scalar.activation(stat[0:1, 3:4], stat[0:1, 2:3], AF.Ln))
                cQs.wait(nc.scalar)
                cQs.inc(nc.scalar.activation(stat[0:1, 4:5], stat[0:1, 3:4], AF.Exp, scale=0.5))
                cQs.wait(nc.vector)
                cQv.inc(nc.vector.tensor_tensor(stat[0:1, 5:6], stat[0:1, 4:5],
                                                stat[0:1, 12:13], OP.max))
                cQv.wait(nc.vector)
                cQv.inc(nc.vector.tensor_tensor(stat[0:1, 5:6], stat[0:1, 5:6],
                                                stat[0:1, 8:9], OP.mult))
                cQv.wait(nc.vector)
                cQv.inc(nc.vector.tensor_tensor(stat[0:1, 5:6], stat[0:1, 5:6],
                                                stat[0:1, 10:11], OP.add))
                cQv.wait(nc.vector)
                cQv.inc(nc.vector.tensor_scalar(stat[0:1, 5:6], stat[0:1, 5:6], 1.0, None, OP.max))
                cQv.wait(nc.vector)
                cQv.inc(nc.vector.tensor_tensor(stat[0:1, 6:7], stat[0:1, 5:6],
                                                stat[0:1, 9:10], OP.mult))
                cQv.wait(nc.vector)
                cQv.inc(nc.vector.tensor_tensor(stat[0:1, 6:7], stat[0:1, 6:7],
                                                stat[0:1, 11:12], OP.add))
                cQv.wait(nc.vector)
                cQv.inc(nc.vector.tensor_tensor(stat[0:1, 7:8], stat[0:1, 4:5],
                                                stat[0:1, 6:7], OP.add))
                cQv.wait(nc.vector)
                cQv.inc(nc.vector.tensor_copy(big2[0:1, 8:9], stat[0:1, 7:8]))
                cQv.wait(nc.tensor)
                cQt.inc(nc.tensor.matmul(psA[:, 600:601], onesb[0:1, 0:128],
                                         big2[0:1, 8:9], start=True, stop=True))
                cQt.wait(nc.gpsimd)
                dbat(cQg, nc.gpsimd)
                for g in range(8):
                    cQg.inc(nc.gpsimd.dma_start(out=ebias[16 * g:16 * (g + 1), 0:1],
                                                in_=big3[0:16, 49:50]), 16)
                cQg.wait(nc.vector)
                cQv.inc(nc.vector.tensor_tensor(ebias[:, 0:1], ebias[:, 0:1],
                                                psA[:, 600:601], OP.subtract))
                if dbg_stat is not None:
                    cQv.wait(nc.sync)
                    dbat(cQd, nc.sync)
                    cQd.inc(nc.sync.dma_start(out=dbg_stat.ap(), in_=stat[0:1, 0:13]), 16)
                    cQd.inc(nc.sync.dma_start(out=dbg_eb.ap(), in_=ebias[0:64, :]), 16)

        es3.close()
        avbuf = es.enter_context(nc.sbuf_tensor("avbuf", [65, H * TI], F32))

        with ExitStack() as esa:
            sba = lambda n_, s, d: esa.enter_context(nc.sbuf_tensor(n_, s, d))
            wxb = sba("wxb", [128, 8192], BF16)
            x2t = sba("x2t", [128, 16384], BF16)
            ktv2 = [sba("ktv0", [128, 1024], BF16), sba("ktv1", [128, 1024], BF16),
                    sba("ktv2b", [128, 1024], BF16)]
            vsb2 = [sba("vsb0", [128, H * 65], BF16), sba("vsb1", [128, H * 65], BF16),
                    sba("vsb2b", [128, H * 65], BF16)]
            qtb = sba("qtb", [128, 8 * TI], BF16)
            sKV = [C("sKV0"), C("sKV1"), C("sKV2")]
            sXSa = C("sXSa"); sXSb = C("sXSb"); sXSc = C("sXSc")
            sTIa = C("sTIa"); sTIb = C("sTIb"); sTIc = C("sTIc")

            SB = 0
            XB = 8192
            BP = 24576
            psV2 = [psFd[0:65, 0:512], psFd[0:65, 512:1024]]
            psAh = [psA[:, 0:512], psA[:, 512:1024]]
            psSp = [psS[:, 0:1024], psS[:, 1024:2048]]

            bk = {"kv_free": [0, 0], "kv_tot": [0, 0], "x_free": 0,
                  "sx_rec": (0, 0, 0), "ti_rec": (0, 0, 0), "w2_free_t": 0,
                  "w2_free_v": 0, "av_done": 0, "ti_mark": [(0, 0)] * 16,
                  "mmax": 4, "evac_marks": [0] * 8, "evac_last": 0}
            bk["kv_free"] = [0, 0, 0]
            bk["kv_tot"] = [0, 0, 0]
            bk["x_free2"] = [0, 0]
            sTI3g = (sTIa, sTIb, sTIc)
            q3g = (nc.sync, nc.scalar, nc.gpsimd)

            def issue_kv(m):
                par = m % 3
                nc.sync.wait_ge(cAt.sem, bk["kv_free"][par])
                dbat(sKV[par], nc.sync)
                sKV[par].inc(nc.sync.dma_start(
                    out=ktv2[par][0:64, :],
                    in_=bass.AP(kTs, (m * H) * 8192, [[128, 64], [2 * 8192, 8], [1, 128]])), 16)
                sKV[par].inc(nc.sync.dma_start(
                    out=ktv2[par][64:128, :],
                    in_=bass.AP(kTs, (m * H + 1) * 8192, [[128, 64], [2 * 8192, 8], [1, 128]])), 16)
                sKV[par].inc(nc.sync.dma_start(out=vsb2[par][:, :], in_=vs[m]), 16)
                bk["kv_tot"][par] = sKV[par].v

            tp_count = [0]

            def sc_compute(tglob, m):
                par = m % 3
                ktv = ktv2[par]
                nc.tensor.wait_ge(sKV[par].sem, bk["kv_tot"][par])
                nc.vector.wait_ge(sXSa.sem, bk["sx_rec"][0])
                nc.vector.wait_ge(sXSb.sem, bk["sx_rec"][1])
                nc.vector.wait_ge(sXSc.sem, bk["sx_rec"][2])
                prev_marks = bk["evac_marks"]
                marks = []
                for hp in range(8):
                    pp = psSp[hp % 2]
                    if hp >= 2:
                        nc.tensor.wait_ge(cAv.sem, marks[hp - 2])
                    else:
                        nc.tensor.wait_ge(cAv.sem, prev_marks[6 + hp])
                    cAt.inc(nc.tensor.matmul(pp[:, 0:512], ktv[0:64, 128 * hp:128 * (hp + 1)],
                                             qtb[0:64, 512 * hp:512 * (hp + 1)],
                                             start=True, stop=True, tile_position=(0, 0)))
                    cAt.inc(nc.tensor.matmul(pp[:, 512:1024], ktv[64:128, 128 * hp:128 * (hp + 1)],
                                             qtb[64:128, 512 * hp:512 * (hp + 1)],
                                             start=True, stop=True, tile_position=(64, 0)))
                    cAt.wait(nc.vector)
                    cAv.inc(nc.vector.tensor_copy(big0[:, SB + 1024 * hp:SB + 1024 * (hp + 1)],
                                                  pp[:, :]))
                    marks.append(cAv.v)
                bk["evac_marks"] = marks
                bk["evac_last"] = cAv.v

            def sc_dma(tglob, m):
                delta = 512 * tglob - 128 * m
                xpar = m % 2
                dbat(sXSa, nc.sync)
                dbat(sXSb, nc.scalar)
                dbat(sXSc, nc.gpsimd)
                for q_ in q3g:
                    q_.wait_ge(cAt.sem, bk["x_free2"][xpar])
                    q_.wait_ge(cAv.sem, bk["evac_last"])
                if xpar == 0:
                    def xap(g, h):
                        return bass.AP(big0, (32 * g + h) * BP + XB,
                                       [[BP, 1], [512, 32], [1, 512]])
                    def xkap(g, bh):
                        return bass.AP(big0, (32 * g + 16) * BP + XB + 512 * 16 * bh,
                                       [[BP, 16], [512, 16], [1, 512]])
                else:
                    def xap(g, h):
                        return bass.AP(x2t, (32 * g + h) * 16384,
                                       [[16384, 1], [512, 32], [1, 512]])
                    def xkap(g, bh):
                        return bass.AP(x2t, (32 * g + 16) * 16384 + 512 * 16 * bh,
                                       [[16384, 16], [512, 16], [1, 512]])
                for hp in range(8):
                    for hh in range(2):
                        h = 2 * hp + hh
                        for g in range(4):
                            sm, q_ = ((sXSa, nc.sync),
                                      (sXSc, nc.gpsimd))[(8 * hp + 4 * hh + g) % 2]
                            sm.inc(q_.dma_start(
                                out=xap(g, h),
                                in_=big0[32 * g:32 * (g + 1),
                                         SB + 512 * h:SB + 512 * (h + 1)]), 16)
                for g in range(4):
                    for bh in range(2):
                        sm, q_ = ((sXSa, nc.sync), (sXSc, nc.gpsimd),
                                  (sXSc, nc.gpsimd), (sXSa, nc.sync),
                                  (sXSc, nc.gpsimd), (sXSa, nc.sync),
                                  (sXSa, nc.sync), (sXSc, nc.gpsimd))[2 * g + bh]
                        sm.inc(q_.dma_start(
                            out=xkap(g, bh),
                            in_=bass.AP(ktd, 16 * g * WKT + (delta + PADK) - 16 * bh,
                                        [[WKT, 16], [-1, 16], [1, 512]])), 16)
                bk["sx_rec"] = (sXSa.v, sXSb.v, sXSc.v)

            def mlp_stage(tglob, m):
                par = m % 2
                delta = 512 * tglob - 128 * m
                nc.tensor.wait_ge(sXSa.sem, bk["sx_rec"][0])
                nc.tensor.wait_ge(sXSb.sem, bk["sx_rec"][1])
                nc.tensor.wait_ge(sXSc.sem, bk["sx_rec"][2])
                g_base = cAs.v
                xv = (lambda b: big0[:, XB + 512 * b:XB + 512 * (b + 1)]) \
                    if m % 2 == 0 else (lambda b: x2t[:, 512 * b:512 * (b + 1)])
                for b in range(32):
                    if b >= 2:
                        nc.tensor.wait_ge(cAs.sem, g_base + b - 1)
                    cAt.inc(nc.tensor.matmul(
                        psAh[b % 2], w1bd[:, :], xv(b),
                        start=True, stop=True))
                    cAt.wait(nc.scalar)
                    cAs.inc(nc.scalar.activation(big1[:, 512 * b:512 * (b + 1)],
                                                 psAh[b % 2], AF.Gelu, bias=b1p[:, 0:1]))
                e_base = cAs.v
                nc.tensor.wait_ge(cAv.sem, bk["av_done"])
                nc.scalar.wait_ge(sTIa.sem, bk["ti_rec"][0])
                nc.scalar.wait_ge(sTIb.sem, bk["ti_rec"][1])
                nc.scalar.wait_ge(sTIc.sem, bk["ti_rec"][2])
                for bp in range(16):
                    if bp >= 2:
                        nc.tensor.wait_ge(cAs.sem, e_base + bp - 1)
                    pF = psFd[:, 512 * (bp % 2):512 * (bp % 2) + 512]
                    for e in range(2):
                        b = bp + 16 * e
                        cAt.inc(nc.tensor.matmul(pF[64 * e:64 * (e + 1), :], w2bd[:, :],
                                                 big1[:, 512 * b:512 * (b + 1)],
                                                 start=True, stop=False,
                                                 tile_position=(0, 64 * e)))
                        cAt.inc(nc.tensor.matmul(pF[64 * e:64 * (e + 1), :], ipack[:, :],
                                                 xv(b),
                                                 start=False, stop=True,
                                                 tile_position=(0, 64 * e)))
                    cAt.wait(nc.scalar)
                    cAs.inc(nc.scalar.activation(wxb[:, 512 * bp:512 * (bp + 1)],
                                                 pF, AF.Exp, bias=ebias[:, 0:1]))
                bk["x_free2"][m % 2] = cAt.v
                w2 = big2
                for q_ in q3g:
                    q_.wait_ge(cAs.sem, cAs.v)
                    q_.wait_ge(cAt.sem, bk["w2_free_t"])
                    q_.wait_ge(cAv.sem, bk["w2_free_v"])
                ti_mark = []
                ti_q = (0, 2, 0, 2, 0, 2, 0, 2, 0, 2, 0, 2, 0, 2, 0, 2)
                for h in range(H):
                    qi = ti_q[h]
                    sm = sTI3g[qi]
                    q_ = q3g[qi]
                    q_.wait_ge(sm.sem, sm.v)
                    for g in range(4):
                        for e in range(2):
                            sm.inc(q_.dma_start(
                                out=w2[32 * g + 16 * e:32 * g + 16 * e + 16,
                                       512 * h:512 * (h + 1)],
                                in_=bass.AP(wxb, (64 * e + 16 * g + h) * 8192,
                                            [[8192, 1], [512, 16], [1, 512]])), 16)
                    ti_mark.append((qi, sm.v))
                bk["ti_mark"] = ti_mark
                bk["ti_rec"] = (sTIa.v, sTIb.v, sTIc.v)
                tp_count[0] += 1

            def av_stage(tglob, m):
                par = m % 3
                vsb = vsb2[par]
                w2 = big2
                diag = m >= 4 * tglob
                o = m - 4 * tglob
                av_base = cAv.v
                ti_mark = bk["ti_mark"]
                for h in range(H):
                    ti, tv = ti_mark[h]
                    if diag:
                        nc.vector.wait_ge(sTI3g[ti].sem, tv)
                        cAv.inc(nc.vector.tensor_tensor(
                            w2[:, 512 * h:512 * (h + 1)], w2[:, 512 * h:512 * (h + 1)],
                            maskb[:, 512 * o:512 * (o + 1)], OP.mult))
                        nc.tensor.wait_ge(cAv.sem, cAv.v)
                    else:
                        nc.tensor.wait_ge(sTI3g[ti].sem, tv)
                        if h >= 2:
                            nc.tensor.wait_ge(cAv.sem, av_base + h - 1)
                    cAt.inc(nc.tensor.matmul(psV2[h % 2], vsb[:, 65 * h:65 * (h + 1)],
                                             w2[:, 512 * h:512 * (h + 1)],
                                             start=True, stop=True))
                    cAt.wait(nc.vector)
                    cAv.inc(nc.vector.tensor_tensor(avbuf[:, 512 * h:512 * (h + 1)],
                                                    avbuf[:, 512 * h:512 * (h + 1)],
                                                    psV2[h % 2], OP.add))
                bk["av_done"] = cAv.v
                bk["w2_free_v"] = cAv.v
                bk["w2_free_t"] = cAt.v
                bk["kv_free"][par] = cAt.v
                if m + 3 < bk["mmax"]:
                    issue_kv(m + 3)

            if cfg.get("phase_a", True):
                cQv.inc(nc.vector.memset(big0[:, XB:XB + 16384], 0.0))
                cQv.inc(nc.vector.memset(x2t[:, :], 0.0))
                barrier([cLD, cQd, cQv, cQt, cQs, cQg])
                fixed = cfg.get("fixed_arm", None)
                if fixed is None:
                    idxs = [nc.values_load(psb_i[0:1, 0:1], engines=[e.engine]) for e in engs]
                    arm_iter = nc.Switch(engines=engs, index=idxs, n=2)
                else:
                    arm_iter = iter([fixed])
                snap = None
                all_ctrs = (cAd, cAt, cAv, cAs, sKV[0], sKV[1], sKV[2],
                            sXSa, sXSb, sXSc, sTIa, sTIb, sTIc)
                for arm in arm_iter:
                    import copy as _copy
                    if arm == 0:
                        snap = (tuple(c.v for c in all_ctrs), _copy.deepcopy(bk))
                    else:
                        for c, v0 in zip(all_ctrs, snap[0]):
                            c.v = v0
                        bk.clear(); bk.update(_copy.deepcopy(snap[1]))
                    for s in range(NSLOT):
                        tglob = ILIST[arm][s]
                        cAd.wait(nc.vector)
                        cAv.inc(nc.vector.memset(avbuf[:, :], 0.0))
                        cAt.wait(nc.sync)
                        dbat(cAd, nc.sync)
                        for oc in range(8):
                            cAd.inc(nc.sync.dma_start(out=qtb[:, 512 * oc:512 * (oc + 1)],
                                                      in_=qTs[s, oc]), 16)
                        cAd.wait(nc.tensor)
                        cAv.wait(nc.tensor)
                        mmax = 4 * (tglob + 1)
                        bk["mmax"] = mmax
                        issue_kv(0)
                        issue_kv(1)
                        issue_kv(2)
                        sc_compute(tglob, 0)
                        sc_dma(tglob, 0)
                        for m in range(mmax):
                            if m + 1 < mmax:
                                sc_compute(tglob, m + 1)
                            if m >= 1:
                                av_stage(tglob, m - 1)
                            if m + 1 < mmax:
                                sc_dma(tglob, m + 1)
                            mlp_stage(tglob, m)
                        av_stage(tglob, mmax - 1)
                        cAv.wait(nc.sync)
                        dbat(cAd, nc.sync)
                        cAd.inc(nc.sync.dma_start(out=avs[s], in_=avbuf[:, :]), 16)
                arm_end = tuple(c.v for c in all_ctrs)

        if cfg.get("phase_f", True):
            barrier([cAd, cAg, cAt, cAv, cAs])
            big3 = es.enter_context(nc.sbuf_tensor("big3f", [128, 8192], F32))
            for kc in range(8):
                cFv.wait(nc.sync)
                dbat(cFd, nc.sync)
                cFd.inc(nc.sync.dma_start(out=big3[:, 0:1024],
                                          in_=w_o[128 * kc:128 * (kc + 1), :]), 16)
                cFd.wait(nc.vector)
                cFv.inc(nc.vector.tensor_copy(big0[:, 1024 * kc:1024 * (kc + 1)],
                                              big3[:, 0:1024]))

            def layernorm(tq):
                xt = big3[:, 1024 * tq:1024 * (tq + 1)]
                mtmp = big3[:, 4096 + 2 * tq:4096 + 2 * tq + 1]
                vtmp = big3[:, 4096 + 2 * tq + 1:4096 + 2 * tq + 2]
                sq = big3[:, 4160:5184]
                cFv.wait(nc.vector)
                cFv.inc(nc.vector.tensor_reduce(mtmp, xt, AX, OP.add))
                cFv.wait(nc.vector)
                cFv.inc(nc.vector.tensor_scalar(mtmp, mtmp, 1.0 / DM, None, OP.mult))
                cFv.wait(nc.vector)
                cFv.inc(nc.vector.tensor_scalar(xt, xt, mtmp, None, OP.subtract))
                cFv.wait(nc.vector)
                cFv.inc(nc.vector.tensor_tensor(sq, xt, xt, OP.mult))
                cFv.wait(nc.vector)
                cFv.inc(nc.vector.tensor_reduce(vtmp, sq, AX, OP.add))
                cFv.wait(nc.vector)
                cFv.inc(nc.vector.tensor_scalar(vtmp, vtmp, 1.0 / DM, EPS, OP.mult, OP.add))
                cFv.wait(nc.scalar)
                cFs.inc(nc.scalar.activation(vtmp, vtmp, AF.Ln))
                cFs.wait(nc.scalar)
                cFs.inc(nc.scalar.activation(vtmp, vtmp, AF.Exp, scale=-0.5))
                cFs.wait(nc.vector)
                cFv.inc(nc.vector.tensor_scalar(xt, xt, vtmp, None, OP.mult))

            for s in range(NSLOT):
                cFv.wait(nc.sync)
                dbat(cFd, nc.sync)
                cFd.inc(nc.sync.dma_start(out=avbuf[:, :], in_=avs[s]), 16)
                cFd.wait(nc.vector)
                nc.vector.wait_ge(cFt.sem, cFt.v)
                cFv.wait(nc.vector)
                for kq in range(4):
                    rr, cc = 32 * (kq // 2), 4096 + 2048 * (kq % 2)
                    cFv.inc(nc.vector.reciprocal(big3[rr:rr + 1, cc:cc + 2048],
                                                 avbuf[64:65, 2048 * kq:2048 * (kq + 1)]))
                cFv.wait(nc.vector)
                for kq in range(4):
                    rr, cc = 32 * (kq // 2), 4096 + 2048 * (kq % 2)
                    cFv.inc(nc.vector.tensor_copy(big2[rr:rr + 1, cc:cc + 2048],
                                                  big3[rr:rr + 1, cc:cc + 2048]))
                cFv.wait(nc.tensor)
                nb = cFv.v
                for h in range(H):
                    kq = h // 4
                    rr, cc = 32 * (kq // 2), 4096 + 2048 * (kq % 2) + 512 * (h % 4)
                    pAv = psA[0:64, 0:512] if h % 2 == 0 else psA[0:64, 512:1024]
                    if h >= 2:
                        nc.tensor.wait_ge(cFv.sem, nb + h - 1)
                    cFt.inc(nc.tensor.matmul(pAv, onesb[rr:rr + 1, 0:64],
                                             big2[rr:rr + 1, cc:cc + 512],
                                             start=True, stop=True))
                    cFt.wait(nc.vector)
                    cFv.inc(nc.vector.tensor_tensor(big2[0:64, 512 * h:512 * (h + 1)],
                                                    avbuf[0:64, 512 * h:512 * (h + 1)],
                                                    pAv, OP.mult))
                cFv.wait(nc.gpsimd)
                dbat(cFg, nc.gpsimd)
                for c in range(8):
                    cFg.inc(nc.gpsimd.dma_start(out=big1[0:64, 512 * c:512 * (c + 1)],
                                                in_=big2[0:64, 1024 * c:1024 * c + 512]), 16)
                    cFg.inc(nc.gpsimd.dma_start(out=big1[64:128, 512 * c:512 * (c + 1)],
                                                in_=big2[0:64, 1024 * c + 512:1024 * (c + 1)]), 16)
                cFg.wait(nc.tensor)
                dbat(cFd, nc.sync)
                dbat(cST, nc.sync)
                for tq in range(4):
                    cFd.inc(nc.sync.dma_start(
                        out=big3[:, 1024 * tq:1024 * (tq + 1)],
                        in_=x_own[512 * s + 128 * tq:512 * s + 128 * (tq + 1), :]), 16)
                cFd.wait(nc.vector)
                cFd.wait(nc.tensor)
                for mc in range(8):
                    for kc in range(8):
                        cFt.inc(nc.tensor.matmul(
                            psA[:, 512:1024],
                            big0[:, (kc * 8 + mc) * 128:(kc * 8 + mc + 1) * 128],
                            big1[:, 512 * kc:512 * (kc + 1)],
                            start=(kc == 0), stop=(kc == 7)))
                    cFt.wait(nc.vector)
                    cFv.inc(nc.vector.tensor_copy(
                        big2[:, 4096 + 512 * mc:4096 + 512 * (mc + 1)], psA[:, 512:1024]))
                    cFv.wait(nc.tensor)
                for mc in range(8):
                    for tq in range(4):
                        cFt.inc(nc.tensor.transpose(
                            psT,
                            big2[:, 4096 + 512 * mc + 128 * tq:4096 + 512 * mc + 128 * (tq + 1)],
                            ident[:, :]))
                        cFt.wait(nc.vector)
                        cFv.inc(nc.vector.tensor_tensor(
                            big3[:, 1024 * tq + 128 * mc:1024 * tq + 128 * (mc + 1)],
                            big3[:, 1024 * tq + 128 * mc:1024 * tq + 128 * (mc + 1)],
                            psT, OP.add))
                        cFv.wait(nc.tensor)
                for tq in range(4):
                    layernorm(tq)
                    cFv.wait(nc.vector)
                    cFv.inc(nc.vector.tensor_copy(big1[:, 1024 * tq:1024 * (tq + 1)],
                                                  big3[:, 1024 * tq:1024 * (tq + 1)]))
                cFv.wait(nc.tensor)
                for tq in range(4):
                    for kc in range(8):
                        cFv.wait(nc.tensor)
                        cFt.inc(nc.tensor.transpose(
                            psT,
                            big1[:, 1024 * tq + 128 * kc:1024 * tq + 128 * (kc + 1)],
                            ident[:, :]))
                        cFt.wait(nc.vector)
                        cFv.inc(nc.vector.tensor_copy(
                            big1[:, 4096 + 512 * kc + 128 * tq:4096 + 512 * kc + 128 * (tq + 1)],
                            psT))
                cFv.wait(nc.sync)
                cFv.wait(nc.tensor)
                sil_base = cFs.v
                mul_base = cFv.v
                wt_rd = [cFt.v, cFt.v]
                for hc in range(24):
                    par = hc % 2
                    sm = (cFd, cFd2)[par]
                    nc.sync.wait_ge(cFt.sem, wt_rd[par])
                    dbat(sm, nc.sync)
                    sm.inc(nc.sync.dma_start(out=big2[:, 2048 * par:2048 * par + 1024],
                                             in_=wgTs[hc].transpose([1, 0, 2])), 16)
                    sm.inc(nc.sync.dma_start(out=big2[:, 2048 * par + 1024:2048 * (par + 1)],
                                             in_=wuTs[hc].transpose([1, 0, 2])), 16)
                    nc.tensor.wait_ge(sm.sem, sm.v)
                    pG = psA[:, 0:512] if par == 0 else psFd[:, 0:512]
                    pU = psA[:, 512:1024] if par == 0 else psFd[:, 512:1024]
                    if hc >= 2:
                        nc.tensor.wait_ge(cFs.sem, sil_base + hc - 1)
                        nc.tensor.wait_ge(cFv.sem, mul_base + hc - 1)
                    for kc in range(8):
                        cFt.inc(nc.tensor.matmul(pG,
                                                 big2[:, 2048 * par + 128 * kc:2048 * par + 128 * (kc + 1)],
                                                 big1[:, 4096 + 512 * kc:4096 + 512 * (kc + 1)],
                                                 start=(kc == 0), stop=(kc == 7)))
                    for kc in range(8):
                        cFt.inc(nc.tensor.matmul(pU,
                                                 big2[:, 2048 * par + 1024 + 128 * kc:2048 * par + 1024 + 128 * (kc + 1)],
                                                 big1[:, 4096 + 512 * kc:4096 + 512 * (kc + 1)],
                                                 start=(kc == 0), stop=(kc == 7)))
                    wt_rd[par] = cFt.v
                    cFt.wait(nc.scalar)
                    cFs.inc(nc.scalar.activation(big2[:, 4096 + 512 * par:4608 + 512 * par],
                                                 pG, AF.Silu, scale=DQ))
                    cFs.wait(nc.vector)
                    cFv.inc(nc.vector.tensor_tensor(big0[:, 8192 + 512 * hc:8192 + 512 * (hc + 1)],
                                                    big2[:, 4096 + 512 * par:4608 + 512 * par],
                                                    pU, OP.mult))
                cFv.wait(nc.tensor)
                cFs.wait(nc.tensor)
                dn_mm = [cFt.v, cFt.v]
                dn_ev = [cFv.v, cFv.v]
                for mc in range(8):
                    par = mc % 2
                    sm = (cFd, cFd2)[par]
                    nc.sync.wait_ge(cFt.sem, dn_mm[par])
                    dbat(sm, nc.sync)
                    sm.inc(nc.sync.dma_start(
                        out=big1[:, 8192 + 3072 * par:8192 + 3072 * par + 3072],
                        in_=wdTs[mc].transpose([1, 0, 2])), 16)
                    nc.tensor.wait_ge(sm.sem, sm.v)
                    nc.tensor.wait_ge(cFv.sem, dn_ev[par])
                    pD = psS[:, 1024 + 512 * par:1536 + 512 * par]
                    for hc in range(24):
                        cFt.inc(nc.tensor.matmul(
                            pD,
                            big1[:, 8192 + 3072 * par + 128 * hc:8192 + 3072 * par + 128 * (hc + 1)],
                            big0[:, 8192 + 512 * hc:8192 + 512 * (hc + 1)],
                            start=(hc == 0), stop=(hc == 23)))
                    dn_mm[par] = cFt.v
                    cFt.wait(nc.vector)
                    cFv.inc(nc.vector.tensor_scalar(big2[:, 1024 + 512 * mc:1024 + 512 * (mc + 1)],
                                                    pD, DQ * DQ, None, OP.mult))
                    dn_ev[par] = cFv.v
                for mc in range(8):
                    for tq in range(4):
                        cFt.inc(nc.tensor.transpose(
                            psT,
                            big2[:, 1024 + 512 * mc + 128 * tq:1024 + 512 * mc + 128 * (tq + 1)],
                            ident[:, :]))
                        cFt.wait(nc.vector)
                        cFv.inc(nc.vector.tensor_tensor(
                            big3[:, 1024 * tq + 128 * mc:1024 * tq + 128 * (mc + 1)],
                            big3[:, 1024 * tq + 128 * mc:1024 * tq + 128 * (mc + 1)],
                            psT, OP.add))
                        cFv.wait(nc.tensor)
                for tq in range(4):
                    layernorm(tq)
                    cFv.wait(nc.sync)
                    cST.inc(nc.sync.dma_start(
                        out=out[512 * s + 128 * tq:512 * s + 128 * (tq + 1), :],
                        in_=big3[:, 1024 * tq:1024 * (tq + 1)]), 16)
            cST.wait(nc.sync)

    return nc


def _rope_tables(positions, scale):
    inv = 1.0 / (10000.0 ** (np.arange(0, Dh, 2) / (Dh * 1.0)))
    ang = positions[None, :] * np.concatenate([inv, inv])[:, None]
    cos = np.cos(ang) * scale
    sin = np.sin(ang) * scale
    ss = np.concatenate([-sin[0:32], sin[32:64]], 0)
    cosT = np.concatenate([cos, cos], 0).astype(np.float32)
    ssT = np.concatenate([ss, ss], 0).astype(np.float32)
    return np.ascontiguousarray(cosT), np.ascontiguousarray(ssT)


def _host_inputs(L0_attn_x, L0_attn_w_qkv, L0_attn_w_o, L0_attn_log_p, L0_attn_log_a,
                 L0_attn_w1, L0_attn_b1, L0_attn_w2, L0_attn_b2,
                 L0_ffn_w_gate, L0_ffn_w_up, L0_ffn_w_down):
    x = np.ascontiguousarray(np.asarray(L0_attn_x, dtype=np.float32))
    ident = np.eye(128, dtype=ml_dtypes.bfloat16)
    ipack = np.zeros((128, 64), dtype=ml_dtypes.bfloat16)
    for g in range(4):
        for hh in range(16):
            ipack[32 * g + hh, 16 * g + hh] = 1
            ipack[32 * g + 16 + hh, 16 * g + hh] = 1
    masks = np.zeros((128, 4 * TI), dtype=ml_dtypes.bfloat16)
    jj = np.arange(128)[:, None]
    iix = np.arange(TI)[None, :]
    for o in range(4):
        masks[:, TI * o:TI * (o + 1)] = (iix >= jj + 128 * o).astype(ml_dtypes.bfloat16)
    gshift = np.array([PADK + 32 * (r // 16) for r in range(64)], np.float32).reshape(64, 1)
    cosk, sink = _rope_tables(np.arange(L, dtype=np.float32), 1.0)

    shared = dict(
        w_qkv=np.ascontiguousarray(np.asarray(L0_attn_w_qkv, np.float32)),
        w_o=np.ascontiguousarray(np.asarray(L0_attn_w_o, np.float32)),
        w1=np.ascontiguousarray(np.asarray(L0_attn_w1, np.float32)),
        b1=np.ascontiguousarray(np.asarray(L0_attn_b1, np.float32).reshape(32, 1)),
        w2=np.ascontiguousarray(np.asarray(L0_attn_w2, np.float32)),
        b2=np.ascontiguousarray(np.asarray(L0_attn_b2, np.float32).reshape(16, 1)),
        log_p=np.ascontiguousarray(np.asarray(L0_attn_log_p, np.float32).reshape(16, 1)),
        log_a=np.ascontiguousarray(np.asarray(L0_attn_log_a, np.float32).reshape(16, 1)),
        wg_i=np.ascontiguousarray(np.asarray(L0_ffn_w_gate, np.int32)),
        wu_i=np.ascontiguousarray(np.asarray(L0_ffn_w_up, np.int32)),
        wd_i=np.ascontiguousarray(np.asarray(L0_ffn_w_down, np.int32)),
        ident_in=ident, ipack_in=ipack, masks_in=masks,
        cosk_in=cosk, sink_in=sink, gshift_in=gshift,
    )
    in_maps = []
    for c in range(8):
        b, half = c // 2, c % 2
        own_rows = np.concatenate([np.arange(512 * t, 512 * (t + 1)) for t in ILIST[half]])
        cosq, sinq = _rope_tables(own_rows.astype(np.float32), 0.125)
        im = dict(shared)
        im.update(x_own=np.ascontiguousarray(x[b][own_rows]),
                  x_kv=np.ascontiguousarray(x[b]),
                  cosq_in=cosq, sinq_in=sinq,
                  par_in=np.array([[half]], np.int32))
        in_maps.append(im)
    return in_maps


def kernel(**inputs):
    in_maps = _host_inputs(**inputs)
    nc = build_program({})
    res = run_bass_kernel_spmd(nc, in_maps, core_ids=list(range(8)))
    outp = np.zeros((B, L, DM), np.float32)
    for c in range(8):
        b, half = c // 2, c % 2
        o = res.results[c]["out"]
        for s, t in enumerate(ILIST[half]):
            outp[b, 512 * t:512 * (t + 1)] = o[512 * s:512 * (s + 1)]
    return outp


# version 15
# speedup vs baseline: 1.1889x; 1.0884x over previous
import os
import numpy as np
import ml_dtypes
import concourse.bass as bass
import concourse.mybir as mybir
from concourse.bass_utils import run_bass_kernel_spmd

F32 = mybir.dt.float32
BF16 = mybir.dt.bfloat16
I32 = mybir.dt.int32
AF = mybir.ActivationFunctionType
OP = mybir.AluOpType
AX = mybir.AxisListType.X

B, L, H, Dh, DM, FF = 4, 4096, 16, 64, 1024, 3072
NT = 2048
NSLOT, TI = 4, 512
ILIST = [[0, 1, 6, 7], [2, 3, 4, 5]]
PADK = 511
WKT = 4704
EPS = 1e-5
DQ = 1.0 / 127.0


class Ctr:
    def __init__(self, es, nc, name):
        self.sem = es.enter_context(nc.semaphore(name))
        self.v = 0

    def inc(self, instr, amt=1):
        instr.then_inc(self.sem, amt)
        self.v += amt
        return instr

    def wait(self, eng, val=None):
        eng.wait_ge(self.sem, self.v if val is None else val)


def build_program(cfg=None):
    cfg = cfg or {}
    nc = bass.Bass("TRN2", target_bir_lowering=False, debug=False)
    dbg = cfg.get("debug", "")
    kq = "ExternalOutput" if "qkv" in dbg else "Internal"
    ka = "ExternalOutput" if "att" in dbg else "Internal"

    x_own = nc.dram_tensor("x_own", [NT, DM], F32, kind="ExternalInput").ap()
    x_kv = nc.dram_tensor("x_kv", [L, DM], F32, kind="ExternalInput").ap()
    w_qkv = nc.dram_tensor("w_qkv", [DM, 3 * DM], F32, kind="ExternalInput").ap()
    w_o = nc.dram_tensor("w_o", [DM, DM], F32, kind="ExternalInput").ap()
    w1 = nc.dram_tensor("w1", [32, 32], F32, kind="ExternalInput").ap()
    b1 = nc.dram_tensor("b1", [32, 1], F32, kind="ExternalInput").ap()
    w2 = nc.dram_tensor("w2", [32, 16], F32, kind="ExternalInput").ap()
    b2 = nc.dram_tensor("b2", [16, 1], F32, kind="ExternalInput").ap()
    log_p = nc.dram_tensor("log_p", [16, 1], F32, kind="ExternalInput").ap()
    log_a = nc.dram_tensor("log_a", [16, 1], F32, kind="ExternalInput").ap()
    wg_i = nc.dram_tensor("wg_i", [FF, DM], I32, kind="ExternalInput").ap()
    wu_i = nc.dram_tensor("wu_i", [FF, DM], I32, kind="ExternalInput").ap()
    wd_i = nc.dram_tensor("wd_i", [DM, FF], I32, kind="ExternalInput").ap()
    ident_in = nc.dram_tensor("ident_in", [128, 128], BF16, kind="ExternalInput").ap()
    ipack_in = nc.dram_tensor("ipack_in", [128, 64], BF16, kind="ExternalInput").ap()
    masks_in = nc.dram_tensor("masks_in", [128, 4 * TI], BF16, kind="ExternalInput").ap()
    cosq_in = nc.dram_tensor("cosq_in", [128, NT], F32, kind="ExternalInput").ap()
    sinq_in = nc.dram_tensor("sinq_in", [128, NT], F32, kind="ExternalInput").ap()
    cosk_in = nc.dram_tensor("cosk_in", [128, L], F32, kind="ExternalInput").ap()
    sink_in = nc.dram_tensor("sink_in", [128, L], F32, kind="ExternalInput").ap()
    gshift_in = nc.dram_tensor("gshift_in", [64, 1], F32, kind="ExternalInput").ap()
    par_in = nc.dram_tensor("par_in", [1, 1], I32, kind="ExternalInput").ap()
    out = nc.dram_tensor("out", [NT, DM], F32, kind="ExternalOutput").ap()

    qTs = nc.dram_tensor("qTs", [NSLOT, 8, 128, TI], BF16, kind=kq)
    kTs = nc.dram_tensor("kTs", [32, H, 64, 128], BF16, kind=kq)
    vs = nc.dram_tensor("vs", [32, 128, H * 65], BF16, kind=kq)
    avs = nc.dram_tensor("avs", [NSLOT, 65, H * TI], F32, kind=ka)
    dbg_stat = nc.dram_tensor("dbg_stat", [1, 13], F32, kind="ExternalOutput") if "stats" in dbg else None
    dbg_eb = nc.dram_tensor("dbg_eb", [64, 1], F32, kind="ExternalOutput") if "stats" in dbg else None
    dbg_wx = nc.dram_tensor("dbg_wx", [64, 16384], BF16, kind="ExternalOutput") if "stats" in dbg else None
    dbg_s = nc.dram_tensor("dbg_s", [128, 8192], BF16, kind="ExternalOutput") if "stats" in dbg else None
    wdTs = nc.dram_tensor("wdTs", [8, 24, 128, 128], BF16, kind="Internal")
    wgTs = nc.dram_tensor("wgTs", [24, 8, 128, 128], BF16, kind="Internal")
    wuTs = nc.dram_tensor("wuTs", [24, 8, 128, 128], BF16, kind="Internal")

    from contextlib import ExitStack
    with ExitStack() as es:
        sb = lambda n_, s, d: es.enter_context(nc.sbuf_tensor(n_, s, d))
        pst_ = lambda n_, s, d: es.enter_context(nc.psum_tensor(n_, s, d))
        C = lambda n_: Ctr(es, nc, n_)

        ident = sb("ident", [128, 128], BF16)
        ipack = sb("ipack", [128, 64], BF16)
        maskb = sb("maskb", [128, 4 * TI], BF16)
        w1bd = sb("w1bd", [128, 128], BF16)
        w1kbd = sb("w1kbd", [64, 128], BF16)
        w2bd = sb("w2bd", [128, 64], BF16)
        b1p = sb("b1p", [128, 1], F32)
        ebias = sb("ebias", [128, 1], F32)
        ktd = nc.dram_tensor("ktd", [64, WKT], BF16, kind="Internal")
        smalls = sb("smalls", [64, 4], F32)
        stat = sb("stat", [4, 16], F32)
        onesb = sb("onesb", [128, 128], BF16)
        psb_i = sb("psb_i", [1, 1], I32)
        big0 = sb("big0", [128, 24576], BF16)
        big1 = sb("big1", [128, 16384], BF16)
        big2 = sb("big2", [128, 8192], BF16)
        es3 = ExitStack()
        big3 = es3.enter_context(nc.sbuf_tensor("big3", [128, 8192], F32))
        kt = es3.enter_context(nc.sbuf_tensor("kt", [64, WKT], BF16))

        psS = pst_("psS", [128, 2048], F32)
        psA = pst_("psA", [128, 1024], F32)
        psFd = pst_("psFd", [128, 1024], F32)
        psS0 = psS[:, 0:512]
        psS1 = psS[:, 512:1024]
        psT = psS[:, 0:64].bitcast(BF16)
        psT64 = psS[:, 512:544].bitcast(BF16)
        psT64b = psS[:, 64:96].bitcast(BF16)
        psT2 = psS[:, 576:640].bitcast(BF16)

        cLD = C("cLD"); cST = C("cST")
        cQd = C("cQd"); cQv = C("cQv"); cQt = C("cQt"); cQs = C("cQs"); cQg = C("cQg")
        cAd = C("cAd"); cAg = C("cAg"); cAt = C("cAt"); cAv = C("cAv"); cAs = C("cAs")
        cFd = C("cFd"); cFd2 = C("cFd2"); cFt = C("cFt"); cFv = C("cFv"); cFs = C("cFs"); cFg = C("cFg")
        cPi = C("cPi")

        engs = [nc.tensor, nc.vector, nc.scalar, nc.sync, nc.gpsimd]

        def barrier(ctrs):
            for e in engs:
                for c in ctrs:
                    c.wait(e)

        def dbat(ctr, eng):
            eng.wait_ge(ctr.sem, ctr.v)

        for ap_, dst in ((ident_in, ident[:, :]), (ipack_in, ipack[:, :]),
                         (masks_in, maskb[:, :]), (par_in, psb_i[:, :])):
            cLD.inc(nc.sync.dma_start(out=dst, in_=ap_), 16)
        cLD.inc(nc.sync.dma_start(out=big3[0:32, 0:32], in_=w1), 16)
        cLD.inc(nc.sync.dma_start(out=big3[0:32, 32:48], in_=w2), 16)
        cLD.inc(nc.sync.dma_start(out=big3[0:32, 48:49], in_=b1), 16)
        cLD.inc(nc.sync.dma_start(out=big3[0:16, 49:50], in_=b2), 16)
        cLD.inc(nc.sync.dma_start(out=big3[0:16, 50:51], in_=log_p), 16)
        cLD.inc(nc.sync.dma_start(out=big3[0:16, 51:52], in_=log_a), 16)
        cLD.inc(nc.sync.dma_start(out=big3[0:64, 52:53], in_=gshift_in), 16)
        cLD.wait(nc.vector)
        cQv.inc(nc.vector.memset(onesb[:, :], 1.0))
        cQv.inc(nc.vector.memset(w1bd[:, :], 0))
        cQv.inc(nc.vector.memset(w1kbd[:, :], 0))
        cQv.inc(nc.vector.memset(w2bd[:, :], 0))
        cQv.inc(nc.vector.tensor_copy(big1[0:32, 0:32], big3[0:32, 0:32]))
        cQv.inc(nc.vector.tensor_copy(big1[0:32, 32:48], big3[0:32, 32:48]))
        cQv.wait(nc.gpsimd)
        dbat(cQg, nc.gpsimd)
        for g in range(4):
            cQg.inc(nc.gpsimd.dma_start(out=w1bd[32 * g:32 * (g + 1), 32 * g:32 * (g + 1)],
                                        in_=big1[0:32, 0:32]), 16)
            cQg.inc(nc.gpsimd.dma_start(out=w1kbd[16 * g:16 * (g + 1), 32 * g:32 * (g + 1)],
                                        in_=big1[16:32, 0:32]), 16)
            cQg.inc(nc.gpsimd.dma_start(out=w2bd[32 * g:32 * (g + 1), 16 * g:16 * (g + 1)],
                                        in_=big1[0:32, 32:48]), 16)
            cQg.inc(nc.gpsimd.dma_start(out=b1p[32 * g:32 * (g + 1), 0:1],
                                        in_=big3[0:32, 48:49]), 16)
        cQv.inc(nc.vector.tensor_scalar(big3[0:16, 53:54], big3[0:16, 50:51], 0.0, None, OP.max))
        cQv.inc(nc.vector.tensor_scalar(big3[0:16, 54:55], big3[0:16, 51:52], 0.0, None, OP.max))
        cQv.wait(nc.scalar)
        cQs.inc(nc.scalar.activation(big3[0:16, 55:56], big3[0:16, 50:51], AF.Abs))
        cQs.inc(nc.scalar.activation(big3[0:16, 56:57], big3[0:16, 51:52], AF.Abs))
        cQs.wait(nc.scalar)
        cQs.inc(nc.scalar.activation(big3[0:16, 57:58], big3[0:16, 55:56], AF.Exp, scale=-1.0))
        cQs.inc(nc.scalar.activation(big3[0:16, 58:59], big3[0:16, 56:57], AF.Exp, scale=-1.0))
        cQs.wait(nc.scalar)
        cQs.inc(nc.scalar.activation(big3[0:16, 57:58], big3[0:16, 57:58], AF.Ln, bias=1.0))
        cQs.inc(nc.scalar.activation(big3[0:16, 58:59], big3[0:16, 58:59], AF.Ln, bias=1.0))
        cQs.wait(nc.vector)
        cQv.inc(nc.vector.tensor_tensor(big3[0:16, 59:60], big3[0:16, 53:54],
                                        big3[0:16, 57:58], OP.add))
        cQv.inc(nc.vector.tensor_tensor(big3[0:16, 60:61], big3[0:16, 54:55],
                                        big3[0:16, 58:59], OP.add))
        cQv.wait(nc.vector)
        cQv.inc(nc.vector.tensor_scalar(big3[0:16, 61:62], big3[0:16, 59:60], -1.0, None, OP.mult))
        cQv.wait(nc.gpsimd)
        dbat(cQg, nc.gpsimd)
        for g in range(4):
            cQg.inc(nc.gpsimd.dma_start(out=smalls[16 * g:16 * (g + 1), 0:1],
                                        in_=big3[0:16, 61:62]), 16)
            cQg.inc(nc.gpsimd.dma_start(out=smalls[16 * g:16 * (g + 1), 1:2],
                                        in_=big3[0:16, 60:61]), 16)
        cQg.wait(nc.gpsimd)
        KTC = 1024
        cPi.inc(nc.gpsimd.iota(big3[0:64, KTC:KTC + WKT], [[1, WKT]],
                               channel_multiplier=0,
                               allow_small_or_imprecise_dtypes=True))
        cPi.wait(nc.vector)
        cQv.inc(nc.vector.tensor_scalar(big3[0:64, KTC:KTC + WKT], big3[0:64, KTC:KTC + WKT],
                                        big3[0:64, 52:53], None, OP.subtract))
        cQv.wait(nc.scalar)
        cQs.inc(nc.scalar.activation(big3[0:64, KTC:KTC + WKT], big3[0:64, KTC:KTC + WKT],
                                     AF.Abs))
        cQs.wait(nc.vector)
        cQv.inc(nc.vector.tensor_scalar(big3[0:64, KTC:KTC + WKT], big3[0:64, KTC:KTC + WKT],
                                        smalls[0:64, 1:2], None, OP.mult))
        cQv.wait(nc.scalar)
        cQs.inc(nc.scalar.activation(big3[0:64, KTC:KTC + WKT], big3[0:64, KTC:KTC + WKT],
                                     AF.Ln, bias=1.0))
        cQs.wait(nc.vector)
        cQv.inc(nc.vector.tensor_scalar(kt[0:64, 0:WKT], big3[0:64, KTC:KTC + WKT],
                                        smalls[0:64, 0:1], None, OP.mult))
        cQv.wait(nc.gpsimd)
        dbat(cQg, nc.gpsimd)
        cQg.inc(nc.gpsimd.dma_start(out=ktd.ap(), in_=kt[0:64, 0:WKT]), 16)
        cQv.inc(nc.vector.memset(stat[0:1, 0:2], 0.0))
        cQv.inc(nc.vector.memset(big3[0:64, 64:116], 0.0))
        cQv.wait(nc.scalar)
        cQs.inc(nc.scalar.activation(big3[0:32, 64:96], big3[0:32, 0:32], AF.Abs))
        cQs.inc(nc.scalar.activation(big3[0:32, 96:112], big3[0:32, 32:48], AF.Abs))
        cQs.inc(nc.scalar.activation(big3[0:32, 112:113], big3[0:32, 48:49], AF.Abs))
        cQs.inc(nc.scalar.activation(big3[0:16, 113:114], big3[0:16, 49:50], AF.Abs))
        cQv.wait(nc.vector)
        cQv.inc(nc.vector.tensor_reduce(big3[0:64, 114:115], kt[0:64, 0:WKT],
                                        AX, OP.max, apply_absolute_value=True))
        cQs.wait(nc.vector)
        cQv.wait(nc.vector)
        cQv.inc(nc.vector.tensor_copy(big2[0:64, 0:51], big3[0:64, 64:115]))
        cQv.wait(nc.tensor)
        cQt.inc(nc.tensor.matmul(psFd[0:1, 0:32], onesb[0:32, 0:1], big2[0:32, 0:32],
                                 start=True, stop=True))
        cQt.inc(nc.tensor.matmul(psFd[0:1, 32:48], onesb[0:32, 0:1], big2[0:32, 32:48],
                                 start=True, stop=True))
        cQt.inc(nc.tensor.matmul(psFd[0:1, 48:49], onesb[0:32, 0:1], big2[0:32, 48:49],
                                 start=True, stop=True))
        cQt.inc(nc.tensor.matmul(psFd[0:1, 49:50], onesb[0:16, 0:1], big2[0:16, 49:50],
                                 start=True, stop=True))
        cQt.inc(nc.tensor.transpose(psT[0:1, 0:64], big2[0:64, 50:51], ident[0:64, 0:64]))
        cQt.wait(nc.vector)
        cQv.inc(nc.vector.tensor_reduce(stat[0:1, 8:9], psFd[0:1, 0:32], AX, OP.max))
        cQv.inc(nc.vector.tensor_reduce(stat[0:1, 9:10], psFd[0:1, 32:48], AX, OP.max))
        cQv.inc(nc.vector.tensor_copy(stat[0:1, 10:11], psFd[0:1, 48:49]))
        cQv.inc(nc.vector.tensor_copy(stat[0:1, 11:12], psFd[0:1, 49:50]))
        cQv.inc(nc.vector.tensor_reduce(stat[0:1, 12:13], psT[0:1, 0:64], AX, OP.max))
        cQv.wait(nc.tensor)

        cQv.wait(nc.sync)
        for w_in, w_out, wd_layout in ((wg_i, wgTs, False), (wu_i, wuTs, False),
                                       (wd_i, wdTs, True)):
            for hc in range(24):
                dbat(cQd, nc.sync)
                if wd_layout:
                    srcap = w_in[:, 128 * hc:128 * (hc + 1)].rearrange("(m p) c -> p m c", p=128)
                else:
                    srcap = w_in[128 * hc:128 * (hc + 1), :].rearrange("p (m c) -> p m c", c=128)
                cQd.inc(nc.sync.dma_start(
                    out=big3[:, 2048:3072].bitcast(I32).rearrange("p (m c) -> p m c", c=128),
                    in_=srcap), 16)
                cQd.wait(nc.vector)
                cQv.inc(nc.vector.tensor_copy(big1[:, 0:1024], big3[:, 2048:3072].bitcast(I32)))
                cQv.wait(nc.tensor)
                for mc in range(8):
                    pt = psT if mc % 2 == 0 else psT2
                    cQv.wait(nc.tensor, cQv.v - 1)
                    cQt.inc(nc.tensor.transpose(pt, big1[:, 128 * mc:128 * (mc + 1)],
                                                ident[:, :]))
                    cQt.wait(nc.vector)
                    cQv.inc(nc.vector.tensor_copy(big1[:, 1024 + 128 * mc:1024 + 128 * (mc + 1)],
                                                  pt))
                cQv.wait(nc.gpsimd)
                dbat(cQg, nc.gpsimd)
                dst = wdTs[:, hc].transpose([1, 0, 2]) if wd_layout else                     w_out[hc].transpose([1, 0, 2])
                cQg.inc(nc.gpsimd.dma_start(
                    out=dst,
                    in_=big1[:, 1024:2048].rearrange("p (m c) -> p m c", c=128)), 16)
                cQg.wait(nc.sync)
                cQg.wait(nc.vector)

        with ExitStack() as esq:
            sbq = lambda n_, s, d: esq.enter_context(nc.sbuf_tensor(n_, s, d))
            tabf = sbq("tabf", [128, 2 * L], BF16)
            tabq = sbq("tabq", [128, 2 * NT], BF16)
            vstage = sbq("vstage", [128, 4 * H * 65], BF16)

            dbat(cQg, nc.gpsimd)
            for ap_, dst in ((cosq_in, tabq[:, 0:NT]), (sinq_in, tabq[:, NT:2 * NT]),
                             (cosk_in, tabf[:, 0:L]), (sink_in, tabf[:, L:2 * L])):
                cQg.inc(nc.gpsimd.dma_start(out=dst, in_=ap_), 16)
            cQg.wait(nc.vector)
            cLD.wait(nc.vector)
            for tq in range(4):
                for h in range(H):
                    cQv.inc(nc.vector.tensor_copy(
                        vstage[:, 1040 * tq + 65 * h + 64:1040 * tq + 65 * h + 65],
                        onesb[:, 0:1]))
            for kc in range(8):
                cQv.wait(nc.sync)
                dbat(cQd, nc.sync)
                cQd.inc(nc.sync.dma_start(out=big3[:, 0:1536],
                                          in_=w_qkv[128 * kc:128 * (kc + 1), 0:1536]), 16)
                cQd.inc(nc.sync.dma_start(out=big3[:, 1536:3072],
                                          in_=w_qkv[128 * kc:128 * (kc + 1), 1536:3072]), 16)
                cQd.wait(nc.vector)
                cQv.inc(nc.vector.tensor_copy(big0[:, 3072 * kc:3072 * (kc + 1)],
                                              big3[:, 0:3072]))

            def rope_chunk(tab, tcol, dst, pa, par):
                Ltab = tab.shape[1] // 2
                cos = tab[:, tcol:tcol + TI]
                sin = tab[:, Ltab + tcol:Ltab + tcol + TI]
                t1 = big3[:, 4096 + 1024 * par:4608 + 1024 * par]
                t2 = big3[:, 4608 + 1024 * par:5120 + 1024 * par]
                cQv.inc(nc.vector.tensor_tensor(t1, pa, cos, OP.mult))
                for q4 in range(4):
                    r = 32 * q4
                    pr = r + 32 if (q4 % 2 == 0) else r - 32
                    cQv.inc(nc.vector.tensor_tensor(t2[r:r + 32, :],
                                                    pa[pr:pr + 32, :],
                                                    sin[r:r + 32, :], OP.mult))
                cQv.wait(nc.vector)
                cQv.inc(nc.vector.tensor_tensor(dst, t1, t2, OP.add))

            def issue_xload(row0, src_ap, copy_mark):
                nc.sync.wait_ge(cQv.sem, copy_mark)
                dbat(cQd, nc.sync)
                for tq in range(4):
                    cQd.inc(nc.sync.dma_start(
                        out=big3[:, 1024 * tq:1024 * (tq + 1)],
                        in_=src_ap[row0 + 128 * tq:row0 + 128 * (tq + 1), :]), 16)

            def proj_block(row0, src_ap, ocs, kind, blk, nxt):
                cQd.wait(nc.vector)
                cQg.wait(nc.vector)
                cQv.inc(nc.vector.tensor_copy(big1[:, 0:4096], big3[:, 0:4096]))
                if nxt is not None:
                    issue_xload(nxt[0], nxt[1], cQv.v)
                cQv.wait(nc.tensor)
                tm = {}
                for idx in range(32):
                    tq, kc = idx // 8, idx % 8
                    pt = psT if idx % 2 == 0 else psT2
                    if idx >= 2:
                        nc.tensor.wait_ge(cQv.sem, tm[idx - 2])
                    cQt.inc(nc.tensor.transpose(
                        pt, big1[:, 1024 * tq + 128 * kc:1024 * tq + 128 * (kc + 1)],
                        ident[:, :]))
                    cQt.wait(nc.vector)
                    cQv.inc(nc.vector.tensor_copy(
                        big1[:, 4096 + 512 * kc + 128 * tq:4096 + 512 * kc + 128 * (tq + 1)],
                        pt))
                    tm[idx] = cQv.v
                cQv.wait(nc.tensor)
                cQs.wait(nc.vector)
                rm = {}
                for ix, oc in enumerate(ocs):
                    pa = psA[:, 512 * (ix % 2):512 * (ix % 2) + 512]
                    if ix >= 2:
                        nc.tensor.wait_ge(cQv.sem, rm[ix - 2])
                    for kc in range(8):
                        cQt.inc(nc.tensor.matmul(
                            pa, big0[:, 3072 * kc + 128 * oc:3072 * kc + 128 * (oc + 1)],
                            big1[:, 4096 + 512 * kc:4096 + 512 * (kc + 1)],
                            start=(kc == 0), stop=(kc == 7)))
                    cQt.wait(nc.vector)
                    if kind == "q":
                        rope_chunk(tabq, TI * blk, big1[:, 8192 + 512 * ix:8192 + 512 * (ix + 1)], pa, ix % 2)
                    elif ix < 8:
                        rope_chunk(tabf, TI * blk, big1[:, 8192 + 512 * ix:8192 + 512 * (ix + 1)], pa, ix % 2)
                    else:
                        cQv.inc(nc.vector.tensor_copy(
                            big1[:, 12288 + 512 * (ix - 8):12288 + 512 * (ix - 7)], pa))
                    rm[ix] = cQv.v
                srow = 0 if kind == "q" else 1
                cQv.wait(nc.scalar)
                cQs.inc(nc.scalar.activation(big2[:, 0:4096], big1[:, 8192:12288],
                                             AF.Square))
                cQs.wait(nc.tensor)
                bm = {}
                for cix in range(8):
                    pf = psFd[0:1, 512 * (cix % 2):512 * (cix % 2) + 512]
                    if cix >= 2:
                        nc.tensor.wait_ge(cQv.sem, bm[cix - 2])
                    cQt.inc(nc.tensor.matmul(pf, onesb[:, 0:1],
                                             big2[:, 512 * cix:512 * (cix + 1)],
                                             start=True, stop=True))
                    cQt.wait(nc.vector)
                    cQv.inc(nc.vector.tensor_reduce(big3[0:1, 6400 + cix:6401 + cix],
                                                    pf, AX, OP.max))
                    bm[cix] = cQv.v
                cQv.wait(nc.vector)
                cQv.inc(nc.vector.tensor_reduce(big3[0:1, 6416:6417],
                                                big3[0:1, 6400:6408], AX, OP.max))
                cQv.wait(nc.vector)
                cQv.inc(nc.vector.tensor_tensor(stat[0:1, srow:srow + 1],
                                                stat[0:1, srow:srow + 1],
                                                big3[0:1, 6416:6417], OP.max))
                if kind == "kv":
                    vm = {}
                    for vi in range(64):
                        cix, hpar, tq = vi // 8, (vi // 4) % 2, vi % 4
                        h = 2 * cix + hpar
                        pv = psT64 if vi % 2 == 0 else psT64b
                        if vi >= 2:
                            nc.tensor.wait_ge(cQv.sem, vm[vi - 2])
                        cQt.inc(nc.tensor.transpose(
                            pv,
                            big1[64 * hpar:64 * (hpar + 1),
                                 12288 + 512 * cix + 128 * tq:12288 + 512 * cix + 128 * (tq + 1)],
                            ident[64 * hpar:64 * (hpar + 1), 64 * hpar:64 * (hpar + 1)]))
                        cQt.wait(nc.vector)
                        cQv.inc(nc.vector.tensor_copy(
                            vstage[:, 1040 * tq + 65 * h:1040 * tq + 65 * h + 64],
                            pv))
                        vm[vi] = cQv.v
                cQv.wait(nc.gpsimd)
                dbat(cQg, nc.gpsimd)
                if kind == "q":
                    for ix in range(8):
                        cQg.inc(nc.gpsimd.dma_start(
                            out=qTs[blk, ix],
                            in_=big1[:, 8192 + 512 * ix:8192 + 512 * (ix + 1)]), 16)
                else:
                    for cix in range(8):
                        for hpar in range(2):
                            h = 2 * cix + hpar
                            srck = big1[64 * hpar:64 * (hpar + 1),
                                        8192 + 512 * cix:8192 + 512 * (cix + 1)]
                            cQg.inc(nc.gpsimd.dma_start(
                                out=bass.AP(kTs, ((4 * blk) * H + h) * 64 * 128,
                                            [[128, 64], [H * 64 * 128, 4], [1, 128]]),
                                in_=srck.rearrange("p (j c) -> p j c", c=128)), 16)
                    for tq in range(4):
                        cQg.inc(nc.gpsimd.dma_start(out=vs[4 * blk + tq],
                                                    in_=vstage[:, 1040 * tq:1040 * (tq + 1)]), 16)
                cQg.wait(nc.vector)

            if cfg.get("phase_q", True):
                blocks = []
                for kb in range(8):
                    blocks.append((512 * kb, x_kv, list(range(8, 24)), "kv", kb))
                for qb in range(4):
                    blocks.append((512 * qb, x_own, list(range(0, 8)), "q", qb))
                issue_xload(blocks[0][0], blocks[0][1], cQv.v)
                for bi, (row0b, srcb, ocsb, kindb, blkb) in enumerate(blocks):
                    nxt = (blocks[bi + 1][0], blocks[bi + 1][1]) if bi + 1 < len(blocks) else None
                    proj_block(row0b, srcb, ocsb, kindb, blkb, nxt)

                cQv.inc(nc.vector.tensor_tensor(stat[0:1, 2:3], stat[0:1, 0:1],
                                                stat[0:1, 1:2], OP.mult))
                cQv.wait(nc.scalar)
                cQs.inc(nc.scalar.activation(stat[0:1, 3:4], stat[0:1, 2:3], AF.Ln))
                cQs.wait(nc.scalar)
                cQs.inc(nc.scalar.activation(stat[0:1, 4:5], stat[0:1, 3:4], AF.Exp, scale=0.5))
                cQs.wait(nc.vector)
                cQv.inc(nc.vector.tensor_tensor(stat[0:1, 5:6], stat[0:1, 4:5],
                                                stat[0:1, 12:13], OP.max))
                cQv.wait(nc.vector)
                cQv.inc(nc.vector.tensor_tensor(stat[0:1, 5:6], stat[0:1, 5:6],
                                                stat[0:1, 8:9], OP.mult))
                cQv.wait(nc.vector)
                cQv.inc(nc.vector.tensor_tensor(stat[0:1, 5:6], stat[0:1, 5:6],
                                                stat[0:1, 10:11], OP.add))
                cQv.wait(nc.vector)
                cQv.inc(nc.vector.tensor_scalar(stat[0:1, 5:6], stat[0:1, 5:6], 1.0, None, OP.max))
                cQv.wait(nc.vector)
                cQv.inc(nc.vector.tensor_tensor(stat[0:1, 6:7], stat[0:1, 5:6],
                                                stat[0:1, 9:10], OP.mult))
                cQv.wait(nc.vector)
                cQv.inc(nc.vector.tensor_tensor(stat[0:1, 6:7], stat[0:1, 6:7],
                                                stat[0:1, 11:12], OP.add))
                cQv.wait(nc.vector)
                cQv.inc(nc.vector.tensor_tensor(stat[0:1, 7:8], stat[0:1, 4:5],
                                                stat[0:1, 6:7], OP.add))
                cQv.wait(nc.vector)
                cQv.inc(nc.vector.tensor_copy(big2[0:1, 8:9], stat[0:1, 7:8]))
                cQv.wait(nc.tensor)
                cQt.inc(nc.tensor.matmul(psA[:, 600:601], onesb[0:1, 0:128],
                                         big2[0:1, 8:9], start=True, stop=True))
                cQt.wait(nc.gpsimd)
                dbat(cQg, nc.gpsimd)
                for g in range(8):
                    cQg.inc(nc.gpsimd.dma_start(out=ebias[16 * g:16 * (g + 1), 0:1],
                                                in_=big3[0:16, 49:50]), 16)
                cQg.wait(nc.vector)
                cQv.inc(nc.vector.tensor_tensor(ebias[:, 0:1], ebias[:, 0:1],
                                                psA[:, 600:601], OP.subtract))
                if dbg_stat is not None:
                    cQv.wait(nc.sync)
                    dbat(cQd, nc.sync)
                    cQd.inc(nc.sync.dma_start(out=dbg_stat.ap(), in_=stat[0:1, 0:13]), 16)
                    cQd.inc(nc.sync.dma_start(out=dbg_eb.ap(), in_=ebias[0:64, :]), 16)

        es3.close()
        avbuf = es.enter_context(nc.sbuf_tensor("avbuf", [65, H * TI], F32))

        with ExitStack() as esa:
            sba = lambda n_, s, d: esa.enter_context(nc.sbuf_tensor(n_, s, d))
            wxb = sba("wxb", [128, 8192], BF16)
            x2t = sba("x2t", [128, 16384], BF16)
            ktv2 = [sba("ktv0", [128, 1024], BF16), sba("ktv1", [128, 1024], BF16),
                    sba("ktv2b", [128, 1024], BF16)]
            vsb2 = [sba("vsb0", [128, H * 65], BF16), sba("vsb1", [128, H * 65], BF16),
                    sba("vsb2b", [128, H * 65], BF16)]
            qtb = sba("qtb", [128, 8 * TI], BF16)
            sKV = [C("sKV0"), C("sKV1"), C("sKV2")]
            sXSa = C("sXSa"); sXSb = C("sXSb"); sXSc = C("sXSc")
            sTIa = C("sTIa"); sTIb = C("sTIb"); sTIc = C("sTIc")

            SB = 0
            XB = 8192
            BP = 24576
            psV2 = [psFd[0:65, 0:512], psFd[0:65, 512:1024]]
            psAh = [psA[:, 0:512], psA[:, 512:1024]]
            psSp = [psS[:, 0:1024], psS[:, 1024:2048]]

            bk = {"kv_free": [0, 0], "kv_tot": [0, 0], "x_free": 0,
                  "sx_rec": (0, 0, 0), "ti_rec": (0, 0, 0), "w2_free_t": 0,
                  "w2_free_v": 0, "av_done": 0, "ti_mark": [(0, 0)] * 16,
                  "mmax": 4, "evac_marks": [0] * 8, "evac_last": 0}
            bk["kv_free"] = [0, 0, 0]
            bk["kv_tot"] = [0, 0, 0]
            bk["x_free2"] = [0, 0]
            sTI3g = (sTIa, sTIb, sTIc)
            q3g = (nc.sync, nc.scalar, nc.gpsimd)

            def issue_kv(m):
                par = m % 3
                nc.sync.wait_ge(cAt.sem, bk["kv_free"][par])
                dbat(sKV[par], nc.sync)
                sKV[par].inc(nc.sync.dma_start(
                    out=ktv2[par][0:64, :],
                    in_=bass.AP(kTs, (m * H) * 8192, [[128, 64], [2 * 8192, 8], [1, 128]])), 16)
                sKV[par].inc(nc.sync.dma_start(
                    out=ktv2[par][64:128, :],
                    in_=bass.AP(kTs, (m * H + 1) * 8192, [[128, 64], [2 * 8192, 8], [1, 128]])), 16)
                sKV[par].inc(nc.sync.dma_start(out=vsb2[par][:, :], in_=vs[m]), 16)
                bk["kv_tot"][par] = sKV[par].v

            tp_count = [0]

            def sc_compute(tglob, m):
                par = m % 3
                ktv = ktv2[par]
                nc.tensor.wait_ge(sKV[par].sem, bk["kv_tot"][par])
                nc.vector.wait_ge(sXSa.sem, bk["sx_rec"][0])
                nc.vector.wait_ge(sXSb.sem, bk["sx_rec"][1])
                nc.vector.wait_ge(sXSc.sem, bk["sx_rec"][2])
                prev_marks = bk["evac_marks"]
                marks = []
                for hp in range(8):
                    pp = psSp[hp % 2]
                    if hp >= 2:
                        nc.tensor.wait_ge(cAv.sem, marks[hp - 2])
                    else:
                        nc.tensor.wait_ge(cAv.sem, prev_marks[6 + hp])
                    cAt.inc(nc.tensor.matmul(pp[:, 0:512], ktv[0:64, 128 * hp:128 * (hp + 1)],
                                             qtb[0:64, 512 * hp:512 * (hp + 1)],
                                             start=True, stop=True, tile_position=(0, 0)))
                    cAt.inc(nc.tensor.matmul(pp[:, 512:1024], ktv[64:128, 128 * hp:128 * (hp + 1)],
                                             qtb[64:128, 512 * hp:512 * (hp + 1)],
                                             start=True, stop=True, tile_position=(64, 0)))
                    cAt.wait(nc.vector)
                    cAv.inc(nc.vector.tensor_copy(big0[:, SB + 1024 * hp:SB + 1024 * (hp + 1)],
                                                  pp[:, :]))
                    marks.append(cAv.v)
                bk["evac_marks"] = marks
                bk["evac_last"] = cAv.v

            def sc_dma(tglob, m):
                delta = 512 * tglob - 128 * m
                xpar = m % 2
                dbat(sXSa, nc.sync)
                dbat(sXSb, nc.scalar)
                dbat(sXSc, nc.gpsimd)
                for q_ in q3g:
                    q_.wait_ge(cAt.sem, bk["x_free2"][xpar])
                    q_.wait_ge(cAv.sem, bk["evac_last"])
                if xpar == 0:
                    def xap(g, h):
                        return bass.AP(big0, (32 * g + h) * BP + XB,
                                       [[BP, 1], [512, 32], [1, 512]])
                    def xkap(g, bh):
                        return bass.AP(big0, (32 * g + 16) * BP + XB + 512 * 16 * bh,
                                       [[BP, 16], [512, 16], [1, 512]])
                else:
                    def xap(g, h):
                        return bass.AP(x2t, (32 * g + h) * 16384,
                                       [[16384, 1], [512, 32], [1, 512]])
                    def xkap(g, bh):
                        return bass.AP(x2t, (32 * g + 16) * 16384 + 512 * 16 * bh,
                                       [[16384, 16], [512, 16], [1, 512]])
                for hp in range(8):
                    for hh in range(2):
                        h = 2 * hp + hh
                        for g in range(4):
                            sm, q_ = ((sXSa, nc.sync),
                                      (sXSc, nc.gpsimd))[(8 * hp + 4 * hh + g) % 2]
                            sm.inc(q_.dma_start(
                                out=xap(g, h),
                                in_=big0[32 * g:32 * (g + 1),
                                         SB + 512 * h:SB + 512 * (h + 1)]), 16)
                for g in range(4):
                    for bh in range(2):
                        sm, q_ = ((sXSa, nc.sync), (sXSc, nc.gpsimd),
                                  (sXSc, nc.gpsimd), (sXSa, nc.sync),
                                  (sXSc, nc.gpsimd), (sXSa, nc.sync),
                                  (sXSa, nc.sync), (sXSc, nc.gpsimd))[2 * g + bh]
                        sm.inc(q_.dma_start(
                            out=xkap(g, bh),
                            in_=bass.AP(ktd, 16 * g * WKT + (delta + PADK) - 16 * bh,
                                        [[WKT, 16], [-1, 16], [1, 512]])), 16)
                bk["sx_rec"] = (sXSa.v, sXSb.v, sXSc.v)

            def mlp_stage(tglob, m):
                par = m % 2
                delta = 512 * tglob - 128 * m
                nc.tensor.wait_ge(sXSa.sem, bk["sx_rec"][0])
                nc.tensor.wait_ge(sXSb.sem, bk["sx_rec"][1])
                nc.tensor.wait_ge(sXSc.sem, bk["sx_rec"][2])
                g_base = cAs.v
                xv = (lambda b: big0[:, XB + 512 * b:XB + 512 * (b + 1)]) \
                    if m % 2 == 0 else (lambda b: x2t[:, 512 * b:512 * (b + 1)])
                for b in range(32):
                    if b >= 2:
                        nc.tensor.wait_ge(cAs.sem, g_base + b - 1)
                    cAt.inc(nc.tensor.matmul(
                        psAh[b % 2], w1bd[:, :], xv(b),
                        start=True, stop=True))
                    cAt.wait(nc.scalar)
                    cAs.inc(nc.scalar.activation(big1[:, 512 * b:512 * (b + 1)],
                                                 psAh[b % 2], AF.Gelu, bias=b1p[:, 0:1]))
                e_base = cAs.v
                nc.tensor.wait_ge(cAv.sem, bk["av_done"])
                nc.scalar.wait_ge(sTIa.sem, bk["ti_rec"][0])
                nc.scalar.wait_ge(sTIb.sem, bk["ti_rec"][1])
                nc.scalar.wait_ge(sTIc.sem, bk["ti_rec"][2])
                for bp in range(16):
                    if bp >= 2:
                        nc.tensor.wait_ge(cAs.sem, e_base + bp - 1)
                    pF = psFd[:, 512 * (bp % 2):512 * (bp % 2) + 512]
                    for e in range(2):
                        b = bp + 16 * e
                        cAt.inc(nc.tensor.matmul(pF[64 * e:64 * (e + 1), :], w2bd[:, :],
                                                 big1[:, 512 * b:512 * (b + 1)],
                                                 start=True, stop=False,
                                                 tile_position=(0, 64 * e)))
                        cAt.inc(nc.tensor.matmul(pF[64 * e:64 * (e + 1), :], ipack[:, :],
                                                 xv(b),
                                                 start=False, stop=True,
                                                 tile_position=(0, 64 * e)))
                    cAt.wait(nc.scalar)
                    cAs.inc(nc.scalar.activation(wxb[:, 512 * bp:512 * (bp + 1)],
                                                 pF, AF.Exp, bias=ebias[:, 0:1]))
                bk["x_free2"][m % 2] = cAt.v
                w2 = big2
                for q_ in q3g:
                    q_.wait_ge(cAs.sem, cAs.v)
                    q_.wait_ge(cAt.sem, bk["w2_free_t"])
                    q_.wait_ge(cAv.sem, bk["w2_free_v"])
                ti_mark = []
                ti_q = (0, 2, 1, 0, 2, 1, 0, 2, 1, 0, 2, 1, 0, 1, 2, 1)
                for h in range(H):
                    qi = ti_q[h]
                    sm = sTI3g[qi]
                    q_ = q3g[qi]
                    q_.wait_ge(sm.sem, sm.v)
                    for g in range(4):
                        for e in range(2):
                            sm.inc(q_.dma_start(
                                out=w2[32 * g + 16 * e:32 * g + 16 * e + 16,
                                       512 * h:512 * (h + 1)],
                                in_=bass.AP(wxb, (64 * e + 16 * g + h) * 8192,
                                            [[8192, 1], [512, 16], [1, 512]])), 16)
                    ti_mark.append((qi, sm.v))
                bk["ti_mark"] = ti_mark
                bk["ti_rec"] = (sTIa.v, sTIb.v, sTIc.v)
                tp_count[0] += 1

            def av_stage(tglob, m):
                par = m % 3
                vsb = vsb2[par]
                w2 = big2
                diag = m >= 4 * tglob
                o = m - 4 * tglob
                av_base = cAv.v
                ti_mark = bk["ti_mark"]
                for h in range(H):
                    ti, tv = ti_mark[h]
                    if diag:
                        nc.vector.wait_ge(sTI3g[ti].sem, tv)
                        cAv.inc(nc.vector.tensor_tensor(
                            w2[:, 512 * h:512 * (h + 1)], w2[:, 512 * h:512 * (h + 1)],
                            maskb[:, 512 * o:512 * (o + 1)], OP.mult))
                        nc.tensor.wait_ge(cAv.sem, cAv.v)
                    else:
                        nc.tensor.wait_ge(sTI3g[ti].sem, tv)
                        if h >= 2:
                            nc.tensor.wait_ge(cAv.sem, av_base + h - 1)
                    cAt.inc(nc.tensor.matmul(psV2[h % 2], vsb[:, 65 * h:65 * (h + 1)],
                                             w2[:, 512 * h:512 * (h + 1)],
                                             start=True, stop=True))
                    cAt.wait(nc.vector)
                    cAv.inc(nc.vector.tensor_tensor(avbuf[:, 512 * h:512 * (h + 1)],
                                                    avbuf[:, 512 * h:512 * (h + 1)],
                                                    psV2[h % 2], OP.add))
                bk["av_done"] = cAv.v
                bk["w2_free_v"] = cAv.v
                bk["w2_free_t"] = cAt.v
                bk["kv_free"][par] = cAt.v
                if m + 3 < bk["mmax"]:
                    issue_kv(m + 3)

            if cfg.get("phase_a", True):
                cQv.inc(nc.vector.memset(big0[:, XB:XB + 16384], 0.0))
                cQv.inc(nc.vector.memset(x2t[:, :], 0.0))
                barrier([cLD, cQd, cQv, cQt, cQs, cQg])
                fixed = cfg.get("fixed_arm", None)
                if fixed is None:
                    idxs = [nc.values_load(psb_i[0:1, 0:1], engines=[e.engine]) for e in engs]
                    arm_iter = nc.Switch(engines=engs, index=idxs, n=2)
                else:
                    arm_iter = iter([fixed])
                snap = None
                all_ctrs = (cAd, cAt, cAv, cAs, sKV[0], sKV[1], sKV[2],
                            sXSa, sXSb, sXSc, sTIa, sTIb, sTIc)
                for arm in arm_iter:
                    import copy as _copy
                    if arm == 0:
                        snap = (tuple(c.v for c in all_ctrs), _copy.deepcopy(bk))
                    else:
                        for c, v0 in zip(all_ctrs, snap[0]):
                            c.v = v0
                        bk.clear(); bk.update(_copy.deepcopy(snap[1]))
                    for s in range(NSLOT):
                        tglob = ILIST[arm][s]
                        cAd.wait(nc.vector)
                        cAv.inc(nc.vector.memset(avbuf[:, :], 0.0))
                        cAt.wait(nc.sync)
                        dbat(cAd, nc.sync)
                        for oc in range(8):
                            cAd.inc(nc.sync.dma_start(out=qtb[:, 512 * oc:512 * (oc + 1)],
                                                      in_=qTs[s, oc]), 16)
                        cAd.wait(nc.tensor)
                        cAv.wait(nc.tensor)
                        mmax = 4 * (tglob + 1)
                        bk["mmax"] = mmax
                        issue_kv(0)
                        issue_kv(1)
                        issue_kv(2)
                        sc_compute(tglob, 0)
                        sc_dma(tglob, 0)
                        for m in range(mmax):
                            if m + 1 < mmax:
                                sc_compute(tglob, m + 1)
                            if m >= 1:
                                av_stage(tglob, m - 1)
                            if m + 1 < mmax:
                                sc_dma(tglob, m + 1)
                            mlp_stage(tglob, m)
                        av_stage(tglob, mmax - 1)
                        cAv.wait(nc.sync)
                        dbat(cAd, nc.sync)
                        cAd.inc(nc.sync.dma_start(out=avs[s], in_=avbuf[:, :]), 16)
                arm_end = tuple(c.v for c in all_ctrs)

        if cfg.get("phase_f", True):
            barrier([cAd, cAg, cAt, cAv, cAs])
            big3 = es.enter_context(nc.sbuf_tensor("big3f", [128, 8192], F32))
            for kc in range(8):
                cFv.wait(nc.sync)
                dbat(cFd, nc.sync)
                cFd.inc(nc.sync.dma_start(out=big3[:, 0:1024],
                                          in_=w_o[128 * kc:128 * (kc + 1), :]), 16)
                cFd.wait(nc.vector)
                cFv.inc(nc.vector.tensor_copy(big0[:, 1024 * kc:1024 * (kc + 1)],
                                              big3[:, 0:1024]))

            def layernorm(tq):
                xt = big3[:, 1024 * tq:1024 * (tq + 1)]
                mtmp = big3[:, 4096 + 2 * tq:4096 + 2 * tq + 1]
                vtmp = big3[:, 4096 + 2 * tq + 1:4096 + 2 * tq + 2]
                sq = big3[:, 4160:5184]
                cFv.wait(nc.vector)
                cFv.inc(nc.vector.tensor_reduce(mtmp, xt, AX, OP.add))
                cFv.wait(nc.vector)
                cFv.inc(nc.vector.tensor_scalar(mtmp, mtmp, 1.0 / DM, None, OP.mult))
                cFv.wait(nc.vector)
                cFv.inc(nc.vector.tensor_scalar(xt, xt, mtmp, None, OP.subtract))
                cFv.wait(nc.vector)
                cFv.inc(nc.vector.tensor_tensor(sq, xt, xt, OP.mult))
                cFv.wait(nc.vector)
                cFv.inc(nc.vector.tensor_reduce(vtmp, sq, AX, OP.add))
                cFv.wait(nc.vector)
                cFv.inc(nc.vector.tensor_scalar(vtmp, vtmp, 1.0 / DM, EPS, OP.mult, OP.add))
                cFv.wait(nc.scalar)
                cFs.inc(nc.scalar.activation(vtmp, vtmp, AF.Ln))
                cFs.wait(nc.scalar)
                cFs.inc(nc.scalar.activation(vtmp, vtmp, AF.Exp, scale=-0.5))
                cFs.wait(nc.vector)
                cFv.inc(nc.vector.tensor_scalar(xt, xt, vtmp, None, OP.mult))

            for s in range(NSLOT):
                cFv.wait(nc.sync)
                dbat(cFd, nc.sync)
                cFd.inc(nc.sync.dma_start(out=avbuf[:, :], in_=avs[s]), 16)
                cFd.wait(nc.vector)
                nc.vector.wait_ge(cFt.sem, cFt.v)
                cFv.wait(nc.vector)
                for kq in range(4):
                    rr, cc = 32 * (kq // 2), 4096 + 2048 * (kq % 2)
                    cFv.inc(nc.vector.reciprocal(big3[rr:rr + 1, cc:cc + 2048],
                                                 avbuf[64:65, 2048 * kq:2048 * (kq + 1)]))
                cFv.wait(nc.vector)
                for kq in range(4):
                    rr, cc = 32 * (kq // 2), 4096 + 2048 * (kq % 2)
                    cFv.inc(nc.vector.tensor_copy(big2[rr:rr + 1, cc:cc + 2048],
                                                  big3[rr:rr + 1, cc:cc + 2048]))
                cFv.wait(nc.tensor)
                nb = cFv.v
                for h in range(H):
                    kq = h // 4
                    rr, cc = 32 * (kq // 2), 4096 + 2048 * (kq % 2) + 512 * (h % 4)
                    pAv = psA[0:64, 0:512] if h % 2 == 0 else psA[0:64, 512:1024]
                    if h >= 2:
                        nc.tensor.wait_ge(cFv.sem, nb + h - 1)
                    cFt.inc(nc.tensor.matmul(pAv, onesb[rr:rr + 1, 0:64],
                                             big2[rr:rr + 1, cc:cc + 512],
                                             start=True, stop=True))
                    cFt.wait(nc.vector)
                    cFv.inc(nc.vector.tensor_tensor(big2[0:64, 512 * h:512 * (h + 1)],
                                                    avbuf[0:64, 512 * h:512 * (h + 1)],
                                                    pAv, OP.mult))
                cFv.wait(nc.gpsimd)
                dbat(cFg, nc.gpsimd)
                for c in range(8):
                    cFg.inc(nc.gpsimd.dma_start(out=big1[0:64, 512 * c:512 * (c + 1)],
                                                in_=big2[0:64, 1024 * c:1024 * c + 512]), 16)
                    cFg.inc(nc.gpsimd.dma_start(out=big1[64:128, 512 * c:512 * (c + 1)],
                                                in_=big2[0:64, 1024 * c + 512:1024 * (c + 1)]), 16)
                cFg.wait(nc.tensor)
                dbat(cFd, nc.sync)
                dbat(cST, nc.sync)
                for tq in range(4):
                    cFd.inc(nc.sync.dma_start(
                        out=big3[:, 1024 * tq:1024 * (tq + 1)],
                        in_=x_own[512 * s + 128 * tq:512 * s + 128 * (tq + 1), :]), 16)
                cFd.wait(nc.vector)
                cFd.wait(nc.tensor)
                for mc in range(8):
                    for kc in range(8):
                        cFt.inc(nc.tensor.matmul(
                            psA[:, 512:1024],
                            big0[:, (kc * 8 + mc) * 128:(kc * 8 + mc + 1) * 128],
                            big1[:, 512 * kc:512 * (kc + 1)],
                            start=(kc == 0), stop=(kc == 7)))
                    cFt.wait(nc.vector)
                    cFv.inc(nc.vector.tensor_copy(
                        big2[:, 4096 + 512 * mc:4096 + 512 * (mc + 1)], psA[:, 512:1024]))
                    cFv.wait(nc.tensor)
                for mc in range(8):
                    for tq in range(4):
                        cFt.inc(nc.tensor.transpose(
                            psT,
                            big2[:, 4096 + 512 * mc + 128 * tq:4096 + 512 * mc + 128 * (tq + 1)],
                            ident[:, :]))
                        cFt.wait(nc.vector)
                        cFv.inc(nc.vector.tensor_tensor(
                            big3[:, 1024 * tq + 128 * mc:1024 * tq + 128 * (mc + 1)],
                            big3[:, 1024 * tq + 128 * mc:1024 * tq + 128 * (mc + 1)],
                            psT, OP.add))
                        cFv.wait(nc.tensor)
                for tq in range(4):
                    layernorm(tq)
                    cFv.wait(nc.vector)
                    cFv.inc(nc.vector.tensor_copy(big1[:, 1024 * tq:1024 * (tq + 1)],
                                                  big3[:, 1024 * tq:1024 * (tq + 1)]))
                cFv.wait(nc.tensor)
                for tq in range(4):
                    for kc in range(8):
                        cFv.wait(nc.tensor)
                        cFt.inc(nc.tensor.transpose(
                            psT,
                            big1[:, 1024 * tq + 128 * kc:1024 * tq + 128 * (kc + 1)],
                            ident[:, :]))
                        cFt.wait(nc.vector)
                        cFv.inc(nc.vector.tensor_copy(
                            big1[:, 4096 + 512 * kc + 128 * tq:4096 + 512 * kc + 128 * (tq + 1)],
                            psT))
                cFv.wait(nc.sync)
                cFv.wait(nc.tensor)
                sil_base = cFs.v
                mul_base = cFv.v
                wt_rd = [cFt.v, cFt.v]
                for hc in range(24):
                    par = hc % 2
                    sm = (cFd, cFd2)[par]
                    nc.sync.wait_ge(cFt.sem, wt_rd[par])
                    dbat(sm, nc.sync)
                    sm.inc(nc.sync.dma_start(out=big2[:, 2048 * par:2048 * par + 1024],
                                             in_=wgTs[hc].transpose([1, 0, 2])), 16)
                    sm.inc(nc.sync.dma_start(out=big2[:, 2048 * par + 1024:2048 * (par + 1)],
                                             in_=wuTs[hc].transpose([1, 0, 2])), 16)
                    nc.tensor.wait_ge(sm.sem, sm.v)
                    pG = psA[:, 0:512] if par == 0 else psFd[:, 0:512]
                    pU = psA[:, 512:1024] if par == 0 else psFd[:, 512:1024]
                    if hc >= 2:
                        nc.tensor.wait_ge(cFs.sem, sil_base + hc - 1)
                        nc.tensor.wait_ge(cFv.sem, mul_base + hc - 1)
                    for kc in range(8):
                        cFt.inc(nc.tensor.matmul(pG,
                                                 big2[:, 2048 * par + 128 * kc:2048 * par + 128 * (kc + 1)],
                                                 big1[:, 4096 + 512 * kc:4096 + 512 * (kc + 1)],
                                                 start=(kc == 0), stop=(kc == 7)))
                    for kc in range(8):
                        cFt.inc(nc.tensor.matmul(pU,
                                                 big2[:, 2048 * par + 1024 + 128 * kc:2048 * par + 1024 + 128 * (kc + 1)],
                                                 big1[:, 4096 + 512 * kc:4096 + 512 * (kc + 1)],
                                                 start=(kc == 0), stop=(kc == 7)))
                    wt_rd[par] = cFt.v
                    cFt.wait(nc.scalar)
                    cFs.inc(nc.scalar.activation(big2[:, 4096 + 512 * par:4608 + 512 * par],
                                                 pG, AF.Silu, scale=DQ))
                    cFs.wait(nc.vector)
                    cFv.inc(nc.vector.tensor_tensor(big0[:, 8192 + 512 * hc:8192 + 512 * (hc + 1)],
                                                    big2[:, 4096 + 512 * par:4608 + 512 * par],
                                                    pU, OP.mult))
                cFv.wait(nc.tensor)
                cFs.wait(nc.tensor)
                dn_mm = [cFt.v, cFt.v]
                dn_ev = [cFv.v, cFv.v]
                for mc in range(8):
                    par = mc % 2
                    sm = (cFd, cFd2)[par]
                    nc.sync.wait_ge(cFt.sem, dn_mm[par])
                    dbat(sm, nc.sync)
                    sm.inc(nc.sync.dma_start(
                        out=big1[:, 8192 + 3072 * par:8192 + 3072 * par + 3072],
                        in_=wdTs[mc].transpose([1, 0, 2])), 16)
                    nc.tensor.wait_ge(sm.sem, sm.v)
                    nc.tensor.wait_ge(cFv.sem, dn_ev[par])
                    pD = psS[:, 1024 + 512 * par:1536 + 512 * par]
                    for hc in range(24):
                        cFt.inc(nc.tensor.matmul(
                            pD,
                            big1[:, 8192 + 3072 * par + 128 * hc:8192 + 3072 * par + 128 * (hc + 1)],
                            big0[:, 8192 + 512 * hc:8192 + 512 * (hc + 1)],
                            start=(hc == 0), stop=(hc == 23)))
                    dn_mm[par] = cFt.v
                    cFt.wait(nc.vector)
                    cFv.inc(nc.vector.tensor_scalar(big2[:, 1024 + 512 * mc:1024 + 512 * (mc + 1)],
                                                    pD, DQ * DQ, None, OP.mult))
                    dn_ev[par] = cFv.v
                for mc in range(8):
                    for tq in range(4):
                        cFt.inc(nc.tensor.transpose(
                            psT,
                            big2[:, 1024 + 512 * mc + 128 * tq:1024 + 512 * mc + 128 * (tq + 1)],
                            ident[:, :]))
                        cFt.wait(nc.vector)
                        cFv.inc(nc.vector.tensor_tensor(
                            big3[:, 1024 * tq + 128 * mc:1024 * tq + 128 * (mc + 1)],
                            big3[:, 1024 * tq + 128 * mc:1024 * tq + 128 * (mc + 1)],
                            psT, OP.add))
                        cFv.wait(nc.tensor)
                for tq in range(4):
                    layernorm(tq)
                    cFv.wait(nc.sync)
                    cST.inc(nc.sync.dma_start(
                        out=out[512 * s + 128 * tq:512 * s + 128 * (tq + 1), :],
                        in_=big3[:, 1024 * tq:1024 * (tq + 1)]), 16)
            cST.wait(nc.sync)

    return nc


def _rope_tables(positions, scale):
    inv = 1.0 / (10000.0 ** (np.arange(0, Dh, 2) / (Dh * 1.0)))
    ang = positions[None, :] * np.concatenate([inv, inv])[:, None]
    cos = np.cos(ang) * scale
    sin = np.sin(ang) * scale
    ss = np.concatenate([-sin[0:32], sin[32:64]], 0)
    cosT = np.concatenate([cos, cos], 0).astype(np.float32)
    ssT = np.concatenate([ss, ss], 0).astype(np.float32)
    return np.ascontiguousarray(cosT), np.ascontiguousarray(ssT)


def _host_inputs(L0_attn_x, L0_attn_w_qkv, L0_attn_w_o, L0_attn_log_p, L0_attn_log_a,
                 L0_attn_w1, L0_attn_b1, L0_attn_w2, L0_attn_b2,
                 L0_ffn_w_gate, L0_ffn_w_up, L0_ffn_w_down):
    x = np.ascontiguousarray(np.asarray(L0_attn_x, dtype=np.float32))
    ident = np.eye(128, dtype=ml_dtypes.bfloat16)
    ipack = np.zeros((128, 64), dtype=ml_dtypes.bfloat16)
    for g in range(4):
        for hh in range(16):
            ipack[32 * g + hh, 16 * g + hh] = 1
            ipack[32 * g + 16 + hh, 16 * g + hh] = 1
    masks = np.zeros((128, 4 * TI), dtype=ml_dtypes.bfloat16)
    jj = np.arange(128)[:, None]
    iix = np.arange(TI)[None, :]
    for o in range(4):
        masks[:, TI * o:TI * (o + 1)] = (iix >= jj + 128 * o).astype(ml_dtypes.bfloat16)
    gshift = np.array([PADK + 32 * (r // 16) for r in range(64)], np.float32).reshape(64, 1)
    cosk, sink = _rope_tables(np.arange(L, dtype=np.float32), 1.0)

    shared = dict(
        w_qkv=np.ascontiguousarray(np.asarray(L0_attn_w_qkv, np.float32)),
        w_o=np.ascontiguousarray(np.asarray(L0_attn_w_o, np.float32)),
        w1=np.ascontiguousarray(np.asarray(L0_attn_w1, np.float32)),
        b1=np.ascontiguousarray(np.asarray(L0_attn_b1, np.float32).reshape(32, 1)),
        w2=np.ascontiguousarray(np.asarray(L0_attn_w2, np.float32)),
        b2=np.ascontiguousarray(np.asarray(L0_attn_b2, np.float32).reshape(16, 1)),
        log_p=np.ascontiguousarray(np.asarray(L0_attn_log_p, np.float32).reshape(16, 1)),
        log_a=np.ascontiguousarray(np.asarray(L0_attn_log_a, np.float32).reshape(16, 1)),
        wg_i=np.ascontiguousarray(np.asarray(L0_ffn_w_gate, np.int32)),
        wu_i=np.ascontiguousarray(np.asarray(L0_ffn_w_up, np.int32)),
        wd_i=np.ascontiguousarray(np.asarray(L0_ffn_w_down, np.int32)),
        ident_in=ident, ipack_in=ipack, masks_in=masks,
        cosk_in=cosk, sink_in=sink, gshift_in=gshift,
    )
    in_maps = []
    for c in range(8):
        b, half = c // 2, c % 2
        own_rows = np.concatenate([np.arange(512 * t, 512 * (t + 1)) for t in ILIST[half]])
        cosq, sinq = _rope_tables(own_rows.astype(np.float32), 0.125)
        im = dict(shared)
        im.update(x_own=np.ascontiguousarray(x[b][own_rows]),
                  x_kv=np.ascontiguousarray(x[b]),
                  cosq_in=cosq, sinq_in=sinq,
                  par_in=np.array([[half]], np.int32))
        in_maps.append(im)
    return in_maps


def kernel(**inputs):
    in_maps = _host_inputs(**inputs)
    nc = build_program({})
    res = run_bass_kernel_spmd(nc, in_maps, core_ids=list(range(8)))
    outp = np.zeros((B, L, DM), np.float32)
    for c in range(8):
        b, half = c // 2, c % 2
        o = res.results[c]["out"]
        for s, t in enumerate(ILIST[half]):
            outp[b, 512 * t:512 * (t + 1)] = o[512 * s:512 * (s + 1)]
    return outp
